# version 23
# baseline (speedup 1.0000x reference)
import os
import numpy as np
import ml_dtypes
from contextlib import ExitStack

import concourse.bass as bass
import concourse.mybir as mybir
from concourse.bass_utils import run_bass_kernel_spmd

F32 = mybir.dt.float32
BF16 = mybir.dt.bfloat16
AF = mybir.ActivationFunctionType
ALU = mybir.AluOpType
AX = mybir.AxisListType

D = 2048
HEADS = 32
HD = 64
NCORES = 8
OWN = 1024
HALO = 8
NT = 8
TOK = 1152
KWIN = 144
TOKQ = 1056
SW = NT * 2 * KWIN
NEG = -1.0e30
EPS = 1e-5
SEQ = 8192
KPH = int(os.environ.get("KPH", "99"))
CPART = int(os.environ.get("KCP", "99"))

ENGS = ("t", "v", "s", "g", "q")
SEMKEYS = ENGS + ("gd",)



def _copy(e, dst, src):
    if hasattr(e, "tensor_copy"):
        return e.tensor_copy(dst, src)
    return e.copy(dst, src)


QPOOL = 8
GDPOOL = 4
DMA_POOLS = {"q": QPOOL, "gd": GDPOOL}


class Prog:
    def __init__(self):
        self.ops = {k: [] for k in ENGS}
        self.cnt = {k: 0 for k in ENGS}
        self.dma_idx = {"q": 0, "gd": 0}
        self.waited = {k: {} for k in ENGS}

    def emit(self, eng, fn, inc=1):
        self.cnt[eng] += inc
        self.ops[eng].append(("op", fn, inc, eng))
        return self.cnt[eng]

    def dma(self, eng, fn):
        queue = "gd" if eng == "g" else eng
        idx = self.dma_idx[queue]
        self.dma_idx[queue] = idx + 1
        pool = DMA_POOLS[queue]
        semkey = (queue, idx % pool)
        val = 16 * (idx // pool + 1)
        self.ops[eng].append(("op", fn, 16, semkey))
        return (semkey, val)

    def wait(self, eng, on, val=None):
        if isinstance(on, tuple) and val is None:
            on, val = on
        if isinstance(val, tuple):
            raise TypeError("token passed as val")
        if on is None or val is None or val <= 0:
            return
        w = self.waited[eng]
        if w.get(on, 0) >= val:
            return
        w[on] = val
        self.ops[eng].append(("wait", on, val))

    def barrier(self):
        snap = dict(self.cnt)
        dsnap = dict(self.dma_idx)
        for e in ENGS:
            for o in ENGS:
                if o != e:
                    self.wait(e, o, snap[o])
            for queue, pool in DMA_POOLS.items():
                n = dsnap[queue]
                for cls in range(pool):
                    k = (n - cls + pool - 1) // pool
                    self.wait(e, (queue, cls), 16 * k)

    def realize(self, nc, sems):
        ops = {k: list(v) for k, v in self.ops.items()}
        for k in ENGS:
            self.ops[k] = []

        with nc.Block() as blk:
            def runner(key):
                def run(engine):
                    for item in ops[key]:
                        if item[0] == "op":
                            _, fn, inc, semkey = item
                            fn(engine).then_inc(sems[semkey], inc)
                        else:
                            _, on, val = item
                            engine.wait_ge(sems[on], val)
                return run
            blk.tensor(runner("t"))
            blk.vector(runner("v"))
            blk.scalar(runner("s"))
            blk.gpsimd(runner("g"))
            blk.sync(runner("q"))


def build_program():
    nc = bass.Bass("TRN2", target_bir_lowering=False, debug=False)

    x_hbm = nc.dram_tensor("x_shard", [OWN + 2 * HALO, D], F32, kind="ExternalInput").ap()
    qk_strips = nc.dram_tensor("qk_strips", [32, 128, 16, 128], F32, kind="ExternalInput").ap()
    v_w = nc.dram_tensor("v_w", [D, D], F32, kind="ExternalInput").ap()
    proj_wt = nc.dram_tensor("proj_wt", [D, D], F32, kind="ExternalInput").ap()
    ffn_wt = nc.dram_tensor("ffn_wt", [D, D], F32, kind="ExternalInput").ap()
    bias_bc = nc.dram_tensor("bias_bc", [3, 128, D], F32, kind="ExternalInput").ap()
    lnwb_bc = nc.dram_tensor("lnwb_bc", [2, 128, D], BF16, kind="ExternalInput").ap()
    masks = nc.dram_tensor("masks", [2, 128, SW], BF16, kind="ExternalInput").ap()
    ident_in = nc.dram_tensor("ident", [128, 128], BF16, kind="ExternalInput").ap()
    y_hbm = nc.dram_tensor("y_scratch", [OWN, D], F32, kind="Internal").ap()
    sc_scr = nc.dram_tensor("sc_scr", [1], F32, kind="Internal").ap()
    out_hbm = nc.dram_tensor("out", [OWN, D], F32, kind="ExternalOutput").ap()

    p = Prog()
    stack = ExitStack()

    ident = stack.enter_context(nc.sbuf_tensor("identb", [128, 128], BF16))
    stats = stack.enter_context(nc.sbuf_tensor("stats", [128, 80], F32))
    scalev = stack.enter_context(nc.sbuf_tensor("scalev", [128, 1], F32))
    nhalf = stack.enter_context(nc.sbuf_tensor("nhalf", [128, 1], F32))
    accD = stack.enter_context(nc.sbuf_tensor("accD", [128, 32], F32))
    accE = stack.enter_context(nc.sbuf_tensor("accE", [128, 64], F32))
    accL = stack.enter_context(nc.sbuf_tensor("accL", [128, 2], F32))

    def st(col):
        return stats[:, col:col + 1]

    big = ExitStack()
    lnT = big.enter_context(nc.sbuf_tensor("lnT", [128, 16 * TOK], BF16))
    qkv_stack = ExitStack()
    QT = qkv_stack.enter_context(nc.sbuf_tensor("QT", [128, 16 * TOKQ], BF16))
    KT = qkv_stack.enter_context(nc.sbuf_tensor("KT", [128, 16 * TOKQ], BF16))
    VTOK = qkv_stack.enter_context(nc.sbuf_tensor("VTOK", [128, 9 * D], BF16))

    semstack = ExitStack()
    sems = {}
    for k in ENGS:
        sems[k] = semstack.enter_context(nc.semaphore(f"sem_{k}"))
    for queue, pool in DMA_POOLS.items():
        for cls in range(pool):
            sems[(queue, cls)] = semstack.enter_context(
                nc.semaphore(f"sem_{queue}{cls}"))
    if True:

        c_const = p.dma("q", lambda e: e.dma_start(out=ident[:], in_=ident_in))
        p.emit("g", lambda e: e.memset(
            lnT[:].rearrange("p (k c) -> p k c", c=TOK)[:, :, 1040:TOK], 0.0))
        p.emit("g", lambda e: e.memset(nhalf[:], -0.5))
        c_zero = p.cnt["g"]

        def ln_stats(xa, par, sq_dump, n=D):
            c_sum, c_sq, c_mean, c_t1, c_var, c_rstd = (par * 8 + i for i in range(6))
            cs1 = p.emit("s", lambda e: e.activation(sq_dump, xa, AF.Square, accum_out=st(c_sq)))
            cv1 = p.emit("v", lambda e: e.reduce_sum(st(c_sum), xa, axis=AX.X))
            p.wait("v", "v", cv1)
            cv2 = p.emit("v", lambda e: e.tensor_scalar(st(c_mean), st(c_sum), 1.0 / n, None, ALU.mult))
            p.wait("v", "v", cv2)
            cv3 = p.emit("v", lambda e: e.tensor_mul(st(c_t1), st(c_mean), st(c_mean)))
            p.wait("v", "s", cs1)
            cv4 = p.emit("v", lambda e: e.tensor_scalar(st(c_var), st(c_sq), 1.0 / n, None, ALU.mult))
            p.wait("v", "v", max(cv3, cv4))
            cv5 = p.emit("v", lambda e: e.tensor_scalar(
                st(c_var), st(c_var), st(c_t1), EPS, ALU.subtract, ALU.add))
            p.wait("s", "v", cv5)
            cs2 = p.emit("s", lambda e: e.activation(st(c_t1), st(c_var), AF.Ln))
            p.wait("s", "s", cs2)
            p.wait("s", "g", c_zero)
            cs3 = p.emit("s", lambda e: e.activation(st(c_rstd), st(c_t1), AF.Exp, scale=nhalf[:]))
            return c_mean, c_rstd, cs3

        def ln_stats2(specs, presums=None):
            cols = [tuple(par * 8 + i for i in range(6)) for (_, par, _) in specs]
            cs1 = []
            for (xa, par, sqd), cc in zip(specs, cols):
                cs1.append(p.emit("s", lambda e, xa=xa, sqd=sqd, cc=cc: e.activation(
                    sqd, xa, AF.Square, accum_out=st(cc[1]))))
            cv1 = []
            for i, ((xa, par, sqd), cc) in enumerate(zip(specs, cols)):
                if presums is not None:
                    ap, rdy = presums[i]
                    for eng_r, cnt_r in rdy:
                        p.wait("v", eng_r, cnt_r)
                    cv1.append(p.emit("v", lambda e, ap=ap, cc=cc: e.reduce_sum(
                        st(cc[0]), ap, axis=AX.X)))
                else:
                    cv1.append(p.emit("v", lambda e, xa=xa, cc=cc: e.reduce_sum(
                        st(cc[0]), xa, axis=AX.X)))
            cv2 = []
            for i, cc in enumerate(cols):
                p.wait("v", "v", cv1[i])
                cv2.append(p.emit("v", lambda e, cc=cc: e.tensor_scalar(
                    st(cc[2]), st(cc[0]), 1.0 / D, None, ALU.mult)))
            cv3 = []
            for i, cc in enumerate(cols):
                p.wait("v", "v", cv2[i])
                cv3.append(p.emit("v", lambda e, cc=cc: e.tensor_mul(
                    st(cc[3]), st(cc[2]), st(cc[2]))))
            cv4 = []
            for i, cc in enumerate(cols):
                p.wait("v", "s", cs1[i])
                cv4.append(p.emit("v", lambda e, cc=cc: e.tensor_scalar(
                    st(cc[4]), st(cc[1]), 1.0 / D, None, ALU.mult)))
            cv5 = []
            for i, cc in enumerate(cols):
                p.wait("v", "v", max(cv3[i], cv4[i]))
                cv5.append(p.emit("v", lambda e, cc=cc: e.tensor_scalar(
                    st(cc[4]), st(cc[4]), st(cc[3]), EPS, ALU.subtract, ALU.add)))
            cs2 = []
            for i, cc in enumerate(cols):
                p.wait("s", "v", cv5[i])
                cs2.append(p.emit("s", lambda e, cc=cc: e.activation(
                    st(cc[3]), st(cc[4]), AF.Ln)))
            out = []
            for i, cc in enumerate(cols):
                p.wait("s", "s", cs2[i])
                p.wait("s", "g", c_zero)
                cs3 = p.emit("s", lambda e, cc=cc: e.activation(
                    st(cc[5]), st(cc[3]), AF.Exp, scale=nhalf[:]))
                out.append((cc[2], cc[5], cs3))
            return out

        with (
            nc.sbuf_tensor("xbuf", [128, 3 * D], F32) as xbuf,
            nc.sbuf_tensor("lnbf", [128, 3 * D], BF16) as lnbf,
            nc.sbuf_tensor("sqdump", [128, 3 * D], F32) as sqdump,
            nc.psum_tensor([128, 1024], BF16) as tpa0,
            nc.psum_tensor([128, 1024], BF16) as tpa1,
        ):
            tps = [tpa0, tpa1]
            xbuf_rd_v = [0, 0]
            xbuf_rd_s = [0, 0]
            lnbf_rd_t = [0, 0]
            bank_rd = [("v", 0), ("v", 0)]

            def transpose_tile(src_bf, valid_slices, cv_apply, dstT, hook_extra_wait=None):
                last_t = 0
                for half in range(2):
                    bank = tps[half]
                    eng_r, cnt_r = bank_rd[half]
                    p.wait("t", "v", cv_apply)
                    p.wait("t", eng_r, cnt_r)
                    p.wait("t", c_const)
                    for k8 in range(8):
                        kc = half * 8 + k8
                        def mk(kc=kc, k8=k8, bank=bank, src_bf=src_bf):
                            return lambda e: e.transpose(
                                bank[:, k8 * 128:(k8 + 1) * 128],
                                src_bf[:, kc * 128:(kc + 1) * 128], ident[:])
                        last_t = p.emit("t", mk())
                    eng = "v" if half == 0 else "s"
                    p.wait(eng, "t", last_t)
                    p.wait(eng, "g", c_zero)
                    c = 0
                    for (so, do, wd) in valid_slices:
                        def mkevac(half=half, so=so, do=do, wd=wd, bank=bank, dstT=dstT):
                            src = bank[:].rearrange("p (k c) -> p k c", c=128)[:, :, so:so + wd]
                            dst = dstT[:].rearrange("p (k c) -> p k c", c=TOK)[
                                :, half * 8:(half + 1) * 8, do:do + wd]
                            return lambda e: _copy(e, dst, src)
                        c = p.emit(eng, mkevac())
                    bank_rd[half] = (eng, c)
                return last_t

            halo_g = p.emit("g", lambda e: e.memset(xbuf[:, 0:D], 0.0))
            p.wait("q", "g", halo_g)
            hq1 = p.dma("q", lambda e: e.dma_start(out=xbuf[0:8, 0:D], in_=x_hbm[0:8, :]))
            hq2 = p.dma("q", lambda e: e.dma_start(
                out=xbuf[64:72, 0:D], in_=x_hbm[OWN + HALO:OWN + 2 * HALO, :]))
            halo_q = [hq1, hq2]

            xa = xbuf[:, 0:D]
            for tok in halo_q:
                p.wait("s", tok)
                p.wait("v", tok)
            p.wait("s", "g", halo_g)
            p.wait("v", "g", halo_g)
            m_c, r_c, cs_r = ln_stats(xa, 0, sqdump[:, 0:D])
            p.wait("v", "s", cs_r)
            cv_apply = p.emit("v", lambda e, xa=xa, m_c=m_c, r_c=r_c: e.tensor_scalar(
                lnbf[:, 0:D], xa, st(m_c), st(r_c), ALU.subtract, ALU.mult))
            xbuf_rd_v[0] = cv_apply
            xbuf_rd_s[0] = p.cnt["s"]
            lt = transpose_tile(lnbf[:, 0:D], [(0, 0, 8), (64, 1032, 8)], cv_apply, lnT)
            lnbf_rd_t[0] = lt

            groups = [[0, 1, 2], [3, 4, 5], [6, 7]]
            xbuf_rd_v = [xbuf_rd_v[0], 0, 0]
            xbuf_rd_s = [xbuf_rd_s[0], 0, 0]
            lnbf_rd_t = [lnbf_rd_t[0], 0, 0]
            for bs in groups:
                xas, lbs, cqs, pars = [], [], [], []
                for idx, b in enumerate(bs):
                    par = b % 3
                    xa = xbuf[:, par * D:(par + 1) * D]
                    p.wait("q", "v", xbuf_rd_v[par])
                    p.wait("q", "s", xbuf_rd_s[par])
                    cq = p.dma("q", lambda e, b=b, xa=xa: e.dma_start(
                        out=xa, in_=x_hbm[HALO + 128 * b: HALO + 128 * b + 128, :]))
                    xas.append(xa)
                    lbs.append(lnbf[:, par * D:(par + 1) * D])
                    cqs.append(cq)
                    pars.append(par)
                for i in range(len(bs)):
                    p.wait("s", cqs[i])
                    p.wait("v", cqs[i])
                stA = ln_stats2([(xas[i], pars[i], sqdump[:, pars[i] * D:(pars[i] + 1) * D])
                                 for i in range(len(bs))])
                for i, b in enumerate(bs):
                    par = pars[i]
                    m_c, r_c, cs_r = stA[i]
                    p.wait("v", "t", lnbf_rd_t[par])
                    p.wait("v", "s", cs_r)
                    cv_apply = p.emit("v", lambda e, lb=lbs[i], xa=xas[i], m_c=m_c, r_c=r_c:
                                      e.tensor_scalar(lb, xa, st(m_c), st(r_c),
                                                      ALU.subtract, ALU.mult))
                    xbuf_rd_v[par] = cv_apply
                    xbuf_rd_s[par] = p.cnt["s"]
                    lt = transpose_tile(lbs[i], [(0, 8 + 128 * b, 128)], cv_apply, lnT)
                    lnbf_rd_t[par] = lt
            p.barrier()
            p.realize(nc, sems)
            if KPH <= 1:
                return nc

        with (
            nc.sbuf_tensor("wstage", [128, 2 * D], F32) as wstage,
            nc.sbuf_tensor("wbf", [128, 2 * D], BF16) as wbf,
            nc.psum_tensor([128, 512], F32) as qk0,
            nc.psum_tensor([128, 512], F32) as qk1,
            nc.psum_tensor([128, 512], F32) as qk2,
            nc.psum_tensor([128, 512], F32) as qk3,
            nc.psum_tensor([128, 512], F32) as qk4,
            nc.psum_tensor([128, 512], F32) as qk5,
        ):
            banksets = [[qk0, qk1, qk2], [qk3, qk4, qk5]]
            stage_rd_v = [0, 0]
            wbf_rd_t = [0, 0]
            bank_rd = [[("v", 0)] * 3, [("v", 0)] * 3]
            for m in range(32):
                par = m % 2
                is_q = m < 16
                ngroups = [(8, 512), (520, 512)] if is_q else [(0, 512), (512, 512), (1024, 16)]
                p.wait("q", "v", stage_rd_v[par])
                cq = p.dma("q", lambda e, m=m, par=par: e.dma_start(
                    out=wstage[:, par * D:(par + 1) * D],
                    in_=qk_strips[m].rearrange("p k c -> p (k c)")))
                p.wait("v", cq)
                p.wait("v", "t", wbf_rd_t[par])
                cv_cast = p.emit("v", lambda e, par=par: _copy(e, 
                    wbf[:, par * D:(par + 1) * D], wstage[:, par * D:(par + 1) * D]))
                stage_rd_v[par] = cv_cast
                p.wait("t", "v", cv_cast)
                banks = banksets[par]
                for gi in range(len(ngroups)):
                    eng_r, cnt_r = bank_rd[par][gi]
                    p.wait("t", eng_r, cnt_r)
                last_mm = 0
                for kc in range(16):
                    for gi, (c0, wd) in enumerate(ngroups):
                        def mk(kc=kc, gi=gi, c0=c0, wd=wd, par=par, banks=banks):
                            return lambda e: e.matmul(
                                banks[gi][:, 0:wd],
                                wbf[:, par * D + kc * 128: par * D + (kc + 1) * 128],
                                lnT[:, kc * TOK + c0: kc * TOK + c0 + wd],
                                start=(kc == 0), stop=(kc == 15))
                        last_mm = p.emit("t", mk())
                wbf_rd_t[par] = last_mm
                dstT = QT if is_q else KT
                mi = m if is_q else m - 16
                for gi, (c0, wd) in enumerate(ngroups):
                    eng = "v" if (gi % 2 == 0) else "s"
                    p.wait(eng, "t", last_mm)
                    if eng == "s":
                        p.wait(eng, "g", c_zero)
                    def mkevac(gi=gi, c0=c0, wd=wd, mi=mi, dstT=dstT, banks=banks):
                        return lambda e: _copy(e, 
                            dstT[:, mi * TOKQ + c0: mi * TOKQ + c0 + wd], banks[gi][:, 0:wd])
                    c = p.emit(eng, mkevac())
                    bank_rd[par][gi] = (eng, c)
            p.barrier()
            p.realize(nc, sems)
            if KPH <= 2:
                return nc

        with (
            nc.sbuf_tensor("vstage", [128, 2 * 8 * 512], F32) as vstage,
            nc.sbuf_tensor("vwbf", [128, 2 * 16 * 512], BF16) as vwbf,
            nc.psum_tensor([128, 512], F32) as vp0,
            nc.psum_tensor([128, 512], F32) as vp1,
        ):
            vbanks = [vp0, vp1]
            stage_rd_v = [0, 0]
            vwbf_rd_t = [0, 0]
            bank_rd = [("v", 0), ("v", 0)]
            NGV = 16 * 512
            NHV = 8 * 512
            for ng in range(4):
                par = ng % 2
                cast_last = 0
                p.wait("v", "t", vwbf_rd_t[par])
                for hh in range(2):
                    slot = (ng * 2 + hh) % 2
                    p.wait("q", "v", stage_rd_v[slot])
                    cq = p.dma("q", lambda e, ng=ng, slot=slot, hh=hh: e.dma_start(
                        out=vstage[:, slot * NHV:(slot + 1) * NHV],
                        in_=v_w.rearrange("(k pp) o -> pp k o", pp=128)[
                            :, hh * 8:(hh + 1) * 8, ng * 512:(ng + 1) * 512]))
                    p.wait("v", cq)
                    def mkc(par=par, hh=hh, slot=slot):
                        return lambda e: _copy(e,
                            vwbf[:, par * NGV + hh * NHV: par * NGV + (hh + 1) * NHV],
                            vstage[:, slot * NHV:(slot + 1) * NHV])
                    cast_last = p.emit("v", mkc())
                    stage_rd_v[slot] = cast_last
                p.wait("t", "v", cast_last)
                last_mm = 0
                for vt in range(9):
                    bpar = vt % 2
                    eng_r, cnt_r = bank_rd[bpar]
                    p.wait("t", eng_r, cnt_r)
                    for kc in range(16):
                        def mk(vt=vt, kc=kc, par=par, bpar=bpar):
                            return lambda e: e.matmul(
                                vbanks[bpar][:],
                                lnT[:, kc * TOK + 128 * vt: kc * TOK + 128 * vt + 128],
                                vwbf[:, par * NGV + kc * 512: par * NGV + (kc + 1) * 512],
                                start=(kc == 0), stop=(kc == 15))
                        last_mm = p.emit("t", mk())
                    eng = "v" if (vt % 2 == 0) else "s"
                    p.wait(eng, "t", last_mm)
                    if eng == "s":
                        p.wait(eng, "g", c_zero)
                    def mkevac(vt=vt, ng=ng, bpar=bpar):
                        return lambda e: _copy(e, 
                            VTOK[:, vt * D + ng * 512: vt * D + (ng + 1) * 512],
                            vbanks[bpar][:])
                    c = p.emit(eng, mkevac())
                    bank_rd[bpar] = (eng, c)
                vwbf_rd_t[par] = last_mm
            p.barrier()
            p.realize(nc, sems)
            if KPH <= 3:
                return nc

        with (
            nc.sbuf_tensor("maskm", [128, SW], BF16) as maskm,
            nc.sbuf_tensor("maskb", [128, SW], BF16) as maskb,
            nc.sbuf_tensor("tstrip", [128, 2 * SW], F32) as tstrip,
            nc.sbuf_tensor("pstrip", [128, 2 * SW], BF16) as pstrip,
            nc.sbuf_tensor("ptbuf", [128, 4 * 1024], BF16) as ptbuf,
            nc.psum_tensor([128, 512], F32) as sp0,
            nc.psum_tensor([128, 512], F32) as sp1,
            nc.psum_tensor([128, 512], F32) as sp2,
            nc.psum_tensor([128, 512], F32) as sp3,
            nc.psum_tensor([128, 1024], BF16) as tpc0,
            nc.psum_tensor([128, 1024], BF16) as tpc1,
            nc.psum_tensor([128, 512], F32) as oc0,
            nc.psum_tensor([128, 512], F32) as oc1,
        ):
            if CPART >= 1:
                cm1 = p.dma("g", lambda e: e.dma_start(out=maskm[:], in_=masks[0]))
                cm2 = p.dma("g", lambda e: e.dma_start(out=maskb[:], in_=masks[1]))

            sbanks = [sp0, sp1, sp2, sp3]
            tbanks = [tpc0, tpc1]
            obanks = [oc0, oc1]
            s_rd = [("v", 0)] * 4
            t_rd = [("v", 0), ("v", 0)]
            o_rd = [("v", 0), ("v", 0)]
            strip_rd = [{"t": 0, "v": 0, "s": 0, "g": 0} for _ in range(2)]
            pt_rd_t = [0, 0, 0, 0]

            def c_front_a(ph):
                spar = ph % 2
                t_ap = tstrip[:, spar * SW:(spar + 1) * SW]
                ev_last = 0
                for h2 in range(2):
                    if h2 == 1:
                        p.wait("t", "t", p.cnt["t"])
                    lo, hi = h2 * 64, h2 * 64 + 64
                    for b in range(NT):
                        sb = b % 4
                        eng_r, cnt_r = s_rd[sb]
                        p.wait("t", eng_r, cnt_r)
                        def mkmm(ph=ph, b=b, lo=lo, hi=hi, sb=sb):
                            return lambda e: e.matmul(
                                sbanks[sb][:, 0:KWIN],
                                QT[lo:hi, ph * TOKQ + 8 + 128 * b: ph * TOKQ + 8 + 128 * b + 128],
                                KT[lo:hi, ph * TOKQ + 128 * b: ph * TOKQ + 128 * b + KWIN],
                                start=True, stop=True)
                        mm_last = p.emit("t", mkmm())
                        if CPART < 1:
                            continue
                        p.wait("v", "t", mm_last)
                        p.wait("v", cm1)
                        p.wait("v", cm2)
                        p.wait("v", "g", strip_rd[spar]["g"])
                        p.wait("v", "s", strip_rd[spar]["s"])
                        def mkem(b=b, h2=h2, sb=sb, t_ap=t_ap):
                            off = (b * 2 + h2) * KWIN
                            return lambda e: e.tensor_mul(
                                t_ap[:, off:off + KWIN],
                                sbanks[sb][:, 0:KWIN],
                                maskm[:, off:off + KWIN])
                        ev_last = p.emit("v", mkem())
                        s_rd[sb] = ("v", ev_last)
                        if h2 == 1:
                            p.wait("g", "v", ev_last)
                            def mkbias(b=b, t_ap=t_ap):
                                off = b * 2 * KWIN
                                return lambda e: e.tensor_add(
                                    t_ap[:, off:off + 2 * KWIN],
                                    t_ap[:, off:off + 2 * KWIN],
                                    maskb[:, off:off + 2 * KWIN])
                            cg_bias = p.emit("g", mkbias())
                if CPART < 2:
                    return None
                strip_rd[spar]["g"] = cg_bias
                return cg_bias

            def c_front_b(ph, cg_bias):
                spar = ph % 2
                t_ap = tstrip[:, spar * SW:(spar + 1) * SW]
                p_ap = pstrip[:, spar * SW:(spar + 1) * SW]
                negmx = stats[:, 16 + 16 * spar: 32 + 16 * spar]
                rccol = stats[:, 48 + 16 * spar: 64 + 16 * spar]
                if cg_bias is None:
                    return None
                p.wait("v", "g", cg_bias)
                cv_mx = p.emit("v", lambda e, t_ap=t_ap, negmx=negmx: e.reduce_max(
                    negmx, t_ap.rearrange("p (s c) -> p s c", c=KWIN),
                    axis=AX.X, negate=True))
                p.wait("g", "v", cv_mx)
                cg_sub = p.emit("g", lambda e, t_ap=t_ap, negmx=negmx: e.tensor_add(
                    t_ap.rearrange("p (s c) -> p s c", c=KWIN),
                    t_ap.rearrange("p (s c) -> p s c", c=KWIN),
                    negmx[:, :, None].broadcast_to([128, 16, KWIN])))
                strip_rd[spar]["g"] = max(strip_rd[spar]["g"], cg_sub)
                p.wait("s", "g", cg_sub)
                cs_exp = p.emit("s", lambda e, t_ap=t_ap: e.activation(t_ap, t_ap, AF.Exp))
                strip_rd[spar]["s"] = cs_exp
                p.wait("v", "s", cs_exp)
                cv_sum = p.emit("v", lambda e, t_ap=t_ap, negmx=negmx: e.reduce_sum(
                    negmx, t_ap.rearrange("p (s c) -> p s c", c=KWIN), axis=AX.X))
                p.wait("v", "v", cv_sum)
                cv_rc = p.emit("v", lambda e, negmx=negmx, rccol=rccol: e.reciprocal(rccol, negmx))
                p.wait("g", "v", cv_rc)
                p.wait("g", "t", strip_rd[spar]["t"])
                cg_norm = p.emit("g", lambda e, t_ap=t_ap, p_ap=p_ap, rccol=rccol: e.tensor_mul(
                    p_ap.rearrange("p (s c) -> p s c", c=KWIN),
                    t_ap.rearrange("p (s c) -> p s c", c=KWIN),
                    rccol[:, :, None].broadcast_to([128, 16, KWIN])))
                strip_rd[spar]["g"] = max(strip_rd[spar]["g"], cg_norm)
                return cg_norm

            def c_back(ph, cv_norm):
                spar = ph % 2
                p_ap = pstrip[:, spar * SW:(spar + 1) * SW]
                if CPART < 3 or cv_norm is None:
                    return
                units = [(h2, b) for h2 in range(2) for b in range(NT)]
                c_ptes = [0, 0, 0, 0]
                p.wait("t", "g", cv_norm)
                p.wait("t", "t", p.cnt["t"])
                for g4 in range(4):
                    tb = g4 % 2
                    eng_r, cnt_r = t_rd[tb]
                    p.wait("t", eng_r, cnt_r)
                    tr_last = 0
                    for ui in range(4):
                        h2, b = units[g4 * 4 + ui]
                        off = (b * 2 + h2) * KWIN
                        def mktr1(off=off, ui=ui, tb=tb, p_ap=p_ap):
                            return lambda e: e.transpose(
                                tbanks[tb][:, ui * 128:(ui + 1) * 128],
                                p_ap[:, off:off + 128], ident[:])
                        p.emit("t", mktr1())
                        def mktr2(off=off, ui=ui, tb=tb, p_ap=p_ap):
                            return lambda e: e.transpose(
                                tbanks[tb][0:16, 512 + ui * 128: 512 + (ui + 1) * 128],
                                p_ap[:, off + 128:off + KWIN], ident[:])
                        tr_last = p.emit("t", mktr2())
                    strip_rd[spar]["t"] = tr_last
                    p.wait("s", "t", tr_last)
                    p.wait("s", "t", pt_rd_t[g4])
                    def mkpte1(tb=tb, g4=g4):
                        return lambda e: _copy(e,
                            ptbuf[:, g4 * 1024: g4 * 1024 + 512],
                            tbanks[tb][:, 0:512])
                    p.emit("s", mkpte1())
                    def mkpte2(tb=tb, g4=g4):
                        return lambda e: _copy(e,
                            ptbuf[0:16, g4 * 1024 + 512: g4 * 1024 + 1024],
                            tbanks[tb][0:16, 512:1024])
                    c_ptes[g4] = p.emit("s", mkpte2())
                    t_rd[tb] = ("s", c_ptes[g4])
                if CPART < 4:
                    return
                for gi, g4 in enumerate((0, 1, 2, 3)):
                    p.wait("t", "s", c_ptes[g4])
                    if gi in (0, 2):
                        p.wait("t", "t", p.cnt["t"])
                    om_last = 0
                    for ui in range(4):
                        h2, b = units[g4 * 4 + ui]
                        head = 2 * ph + h2
                        ob = b // 4
                        eng_r2, cnt_r2 = o_rd[ob]
                        p.wait("t", eng_r2, cnt_r2)
                        def mko1(ui=ui, h2=h2, b=b, head=head, ob=ob, g4=g4):
                            return lambda e: e.matmul(
                                obanks[ob][h2 * 64:(h2 + 1) * 64, (b % 4) * 128:(b % 4) * 128 + 128],
                                VTOK[:, b * D + head * HD: b * D + head * HD + HD],
                                ptbuf[:, g4 * 1024 + ui * 128: g4 * 1024 + (ui + 1) * 128],
                                start=True, stop=False)
                        p.emit("t", mko1())
                        def mko2(ui=ui, h2=h2, b=b, head=head, ob=ob, g4=g4):
                            return lambda e: e.matmul(
                                obanks[ob][h2 * 64:(h2 + 1) * 64, (b % 4) * 128:(b % 4) * 128 + 128],
                                VTOK[0:16, (b + 1) * D + head * HD: (b + 1) * D + head * HD + HD],
                                ptbuf[0:16, g4 * 1024 + 512 + ui * 128: g4 * 1024 + 512 + (ui + 1) * 128],
                                start=False, stop=True)
                        om_last = p.emit("t", mko2())
                    pt_rd_t[g4] = om_last
                    if CPART >= 5 and g4 >= 2:
                        ob = g4 - 2
                        eng = "s"
                        p.wait(eng, "t", om_last)
                        def mkoe(ob=ob, ph=ph):
                            return lambda e: _copy(e, 
                                lnT[:, ph * TOK + 8 + ob * 512: ph * TOK + 8 + ob * 512 + 512],
                                obanks[ob][:])
                        c_oe = p.emit(eng, mkoe())
                        o_rd[ob] = (eng, c_oe)

            cev = {}
            cnorm = {}
            for k in range(18):
                if k >= 2:
                    c_back(k - 2, cnorm[k - 2])
                if 1 <= k <= 16:
                    cnorm[k - 1] = c_front_b(k - 1, cev[k - 1])
                if k < 16:
                    cev[k] = c_front_a(k)
            p.barrier()
            p.realize(nc, sems)
            if KPH <= 4:
                return nc

        qkv_stack.close()

        with (
            nc.sbuf_tensor("x1", [128, NT * D], F32) as x1,
            nc.sbuf_tensor("projb", [128, D], F32) as projb,
            nc.sbuf_tensor("pwstage", [128, 2 * 8 * 512], F32) as pwstage,
            nc.sbuf_tensor("pwbf", [128, 2 * 16 * 512], BF16) as pwbf,
            nc.sbuf_tensor("xre", [128, 2 * D], F32) as xre,
            nc.sbuf_tensor("sqdump2", [128, 2 * D], F32) as sqdump2,
            nc.psum_tensor([128, 512], F32) as dp0,
            nc.psum_tensor([128, 512], F32) as dp1,
        ):
            cb = p.dma("g", lambda e: e.dma_start(out=projb[:], in_=bias_bc[0]))
            projb_q = cb
            dbanks = [dp0, dp1]
            NGP = 16 * 512
            NHS = 8 * 512
            stage_rd_v = [0, 0]
            pwbf_rd_t = [0, 0]
            bank_rd = [("v", 0), ("v", 0)]
            for ng in range(4):
                par = ng % 2
                cast_last = 0
                p.wait("v", "t", pwbf_rd_t[par])
                for hh in range(2):
                    slot = (ng * 2 + hh) % 2
                    p.wait("q", "v", stage_rd_v[slot])
                    cq = p.dma("q", lambda e, ng=ng, slot=slot, hh=hh: e.dma_start(
                        out=pwstage[:, slot * NHS:(slot + 1) * NHS],
                        in_=proj_wt.rearrange("(k pp) o -> pp k o", pp=128)[
                            :, hh * 8:(hh + 1) * 8, ng * 512:(ng + 1) * 512]))
                    p.wait("v", cq)
                    def mkc(par=par, hh=hh, slot=slot):
                        return lambda e: _copy(e, 
                            pwbf[:, par * NGP + hh * NHS: par * NGP + (hh + 1) * NHS],
                            pwstage[:, slot * NHS:(slot + 1) * NHS])
                    cast_last = p.emit("v", mkc())
                    stage_rd_v[slot] = cast_last
                p.wait("t", "v", cast_last)
                mm_last = 0
                for t in range(NT):
                    bpar = t % 2
                    eng_r, cnt_r = bank_rd[bpar]
                    p.wait("t", eng_r, cnt_r)
                    for kc in range(16):
                        def mk(t=t, kc=kc, par=par, bpar=bpar):
                            return lambda e: e.matmul(
                                dbanks[bpar][:],
                                lnT[:, kc * TOK + 8 + 128 * t: kc * TOK + 8 + 128 * t + 128],
                                pwbf[:, par * NGP + kc * 512: par * NGP + (kc + 1) * 512],
                                start=(kc == 0), stop=(kc == 15))
                        mm_last = p.emit("t", mk())
                    p.wait("v", "t", mm_last)
                    p.wait("v", projb_q)
                    def mkev(t=t, ng=ng, bpar=bpar):
                        return lambda e: e.scalar_tensor_tensor(
                            x1[:, t * D + ng * 512: t * D + (ng + 1) * 512],
                            dbanks[bpar][:], 1.0,
                            projb[:, ng * 512:(ng + 1) * 512],
                            ALU.mult, ALU.add,
                            accum_out=accD[:, t * 4 + ng: t * 4 + ng + 1])
                    c = p.emit("v", mkev())
                    bank_rd[bpar] = ("v", c)
                pwbf_rd_t[par] = mm_last
            proj_done_v = p.cnt["v"]

            xre_rd = [0, 0]
            for tpair in range(0, NT, 2):
                ts = [tpair, tpair + 1]
                xas, xres, cqs = [], [], []
                for t in ts:
                    par = t % 2
                    p.wait("q", "v", xre_rd[par])
                    cq = p.dma("q", lambda e, t=t, par=par: e.dma_start(
                        out=xre[:, par * D:(par + 1) * D],
                        in_=x_hbm[HALO + 128 * t: HALO + 128 * t + 128, :]))
                    xas.append(x1[:, t * D:(t + 1) * D])
                    xres.append(xre[:, par * D:(par + 1) * D])
                    cqs.append(cq)
                p.wait("v", "v", proj_done_v)
                p.wait("s", "v", proj_done_v)
                st4 = ln_stats2([(xas[i], ts[i] % 2,
                                  sqdump2[:, (ts[i] % 2) * D:(ts[i] % 2 + 1) * D]) for i in range(2)],
                                presums=[(accD[:, t * 4:(t + 1) * 4], []) for t in ts])
                for i, t in enumerate(ts):
                    par = t % 2
                    m_c, r_c, cs_r = st4[i]
                    p.wait("v", "s", cs_r)
                    c_c1 = 6 + par * 8
                    cv_c1 = p.emit("v", lambda e, m_c=m_c, r_c=r_c, c_c1=c_c1: e.tensor_mul(
                        st(c_c1), st(m_c), st(r_c)))
                    p.wait("v", cqs[i])
                    p.wait("v", "v", cv_c1)
                    def mke1(xa=xas[i], xr=xres[i], r_c=r_c):
                        return lambda e: e.scalar_tensor_tensor(
                            xa, xa, st(r_c), xr, ALU.mult, ALU.add)
                    cv_e1 = p.emit("v", mke1())
                    p.wait("v", "v", cv_e1)
                    def mke2(xa=xas[i], c_c1=c_c1):
                        return lambda e: e.tensor_scalar(xa, xa, st(c_c1), None, ALU.subtract)
                    cv_e2 = p.emit("v", mke2())
                    xre_rd[par] = cv_e2
                    p.wait("g", "v", cv_e2)
                    p.dma("g", lambda e, t=t, xa=xas[i]: e.dma_start(
                        out=y_hbm[128 * t:128 * t + 128, :], in_=xa))
            p.barrier()
            p.realize(nc, sems)
            if KPH <= 5:
                return nc

        big.close()

        eph = ExitStack()
        ybuf = eph.enter_context(nc.sbuf_tensor("ybuf", [128, 2 * D], F32))
        scr_e = eph.enter_context(nc.sbuf_tensor("scr_e", [128, 2 * D], F32))
        lnbf2 = eph.enter_context(nc.sbuf_tensor("lnbf2", [128, 2 * D], BF16))
        xnT = eph.enter_context(nc.sbuf_tensor("xnT", [128, 16 * 1024], BF16))
        ffnstage = eph.enter_context(nc.sbuf_tensor("ffnstage", [128, NT * D], BF16))
        fwstage = eph.enter_context(nc.sbuf_tensor("fwstage", [128, 2 * 8 * 512], F32))
        fwbf = eph.enter_context(nc.sbuf_tensor("fwbf", [128, 2 * 16 * 512], BF16))
        gam = eph.enter_context(nc.sbuf_tensor("gam", [128, D], F32))
        bet = eph.enter_context(nc.sbuf_tensor("bet", [128, D], F32))
        lnw = eph.enter_context(nc.sbuf_tensor("lnw", [128, D], BF16))
        lnb = eph.enter_context(nc.sbuf_tensor("lnb", [128, D], BF16))
        amaxc = eph.enter_context(nc.sbuf_tensor("amaxc", [128, 16], F32))
        srow = eph.enter_context(nc.sbuf_tensor("srow", [1, 1], F32))
        tpe0 = eph.enter_context(nc.psum_tensor([128, 1024], BF16))
        tpe1 = eph.enter_context(nc.psum_tensor([128, 1024], BF16))
        ep0 = eph.enter_context(nc.psum_tensor([128, 512], F32))
        ep1 = eph.enter_context(nc.psum_tensor([128, 512], F32))
        if True:
            cb_gam = p.dma("g", lambda e: e.dma_start(out=gam[:], in_=bias_bc[1]))
            cb_bet = p.dma("g", lambda e: e.dma_start(out=bet[:], in_=bias_bc[2]))
            cb_lnw = p.dma("g", lambda e: e.dma_start(out=lnw[:], in_=lnwb_bc[0]))
            cb_lnb = p.dma("g", lambda e: e.dma_start(out=lnb[:], in_=lnwb_bc[1]))
            vec_toks = [cb_gam, cb_bet, cb_lnw, cb_lnb]

            xnT_rdy = {}
            NGF = 16 * 512
            NHF = 8 * 512
            stage_rd_s = [0, 0]
            stage_rd_v = [0, 0]
            fwbf_rd_t = [0, 0]
            ebanks = [ep0, ep1]
            ebank_rd = [("v", 0), ("v", 0)]
            estate = {"amax_idx": 0, "slot": 0, "sign_last": {}}
            ffn_rdy = {}

            def emit_ffn_stage(ng):
                par = ng % 2
                sgn_last = 0
                p.wait("s", "t", fwbf_rd_t[par])
                for hh in range(2):
                    slot = estate["slot"] % 2
                    estate["slot"] += 1
                    p.wait("q", "s", stage_rd_s[slot])
                    p.wait("q", "v", stage_rd_v[slot])
                    cq = p.dma("q", lambda e, ng=ng, slot=slot, hh=hh: e.dma_start(
                        out=fwstage[:, slot * NHF:(slot + 1) * NHF],
                        in_=ffn_wt.rearrange("(k pp) o -> pp k o", pp=128)[
                            :, hh * 8:(hh + 1) * 8, ng * 512:(ng + 1) * 512]))
                    p.wait("s", cq)
                    def mks(par=par, hh=hh, slot=slot):
                        return lambda e: e.activation(
                            fwbf[:, par * NGF + hh * NHF: par * NGF + (hh + 1) * NHF],
                            fwstage[:, slot * NHF:(slot + 1) * NHF], AF.Sign)
                    sgn_last = p.emit("s", mks())
                    stage_rd_s[slot] = sgn_last
                    p.wait("v", cq)
                    idx = estate["amax_idx"]
                    estate["amax_idx"] += 1
                    cvm = p.emit("v", lambda e, slot=slot, idx=idx: e.reduce_max(
                        amaxc[:, idx:idx + 1],
                        fwstage[:, slot * NHF:(slot + 1) * NHF],
                        axis=AX.X, apply_absolute_value=True))
                    stage_rd_v[slot] = cvm
                    estate["amax_v"] = cvm
                estate["sign_last"][ng] = sgn_last

            def emit_ffn_mms(ng, tlist):
                par = ng % 2
                p.wait("t", "s", estate["sign_last"][ng])
                p.wait("t", "t", p.cnt["t"])
                mm_last = 0
                for t in tlist:
                    bpar = t % 2
                    eng_r, cnt_r = ebank_rd[bpar]
                    p.wait("t", eng_r, cnt_r)
                    for (eng_x, c_x) in xnT_rdy[t]:
                        p.wait("t", eng_x, c_x)
                    for kc in range(16):
                        def mk(t=t, kc=kc, ng=ng, par=par, bpar=bpar):
                            return lambda e: e.matmul(
                                ebanks[bpar][:],
                                xnT[:, kc * 1024 + 128 * t: kc * 1024 + 128 * t + 128],
                                fwbf[:, par * NGF + kc * 512: par * NGF + (kc + 1) * 512],
                                start=(kc == 0), stop=(kc == 15))
                        mm_last = p.emit("t", mk())
                    eng = "v" if (t % 2 == 0) else "s"
                    p.wait(eng, "t", mm_last)
                    def mkev(t=t, ng=ng, bpar=bpar):
                        return lambda e: _copy(e,
                            ffnstage[:, t * D + ng * 512: t * D + (ng + 1) * 512],
                            ebanks[bpar][:])
                    c = p.emit(eng, mkev())
                    ebank_rd[bpar] = (eng, c)
                    ffn_rdy.setdefault(t, []).append((eng, c))
                fwbf_rd_t[par] = max(fwbf_rd_t[par], mm_last)

            emit_ffn_stage(0)

            tps = [tpe0, tpe1]
            bank_rd = [("v", 0), ("v", 0)]
            lnbf_rd_t = [0, 0]
            scr_rd_g = [0, 0]
            ybuf_rd = [0, 0]
            for tpair in range(0, NT, 2):
                ts = [tpair, tpair + 1]
                yas, scrs, lbs, cqs = [], [], [], []
                for t in ts:
                    par = t % 2
                    ya = ybuf[:, par * D:(par + 1) * D]
                    p.wait("q", "v", ybuf_rd[par])
                    cq = p.dma("q", lambda e, t=t, ya=ya: e.dma_start(
                        out=ya, in_=y_hbm[128 * t:128 * t + 128, :]))
                    yas.append(ya)
                    scrs.append(scr_e[:, par * D:(par + 1) * D])
                    lbs.append(lnbf2[:, par * D:(par + 1) * D])
                    cqs.append(cq)
                for i, t in enumerate(ts):
                    p.wait("v", cqs[i])
                    p.wait("s", cqs[i])
                    p.wait("s", "t", lnbf_rd_t[t % 2])
                st1 = ln_stats2([(yas[i], ts[i] % 2, lbs[i]) for i in range(2)])
                cv_ln2 = []
                for i, t in enumerate(ts):
                    m_c, r_c, cs_r = st1[i]
                    p.wait("v", "s", cs_r)
                    p.wait("v", "g", scr_rd_g[t % 2])
                    cv_ln2.append(p.emit("v", lambda e, scr=scrs[i], ya=yas[i], m_c=m_c, r_c=r_c:
                                         e.tensor_scalar(scr, ya, st(m_c), st(r_c),
                                                         ALU.subtract, ALU.mult)))
                    ybuf_rd[t % 2] = cv_ln2[i]
                for i in range(2):
                    p.wait("s", "v", cv_ln2[i])
                    p.wait("v", "v", cv_ln2[i])
                st2 = ln_stats2([(scrs[i], ts[i] % 2, lbs[i]) for i in range(2)])
                cv_ln3 = []
                for i, t in enumerate(ts):
                    m_c2, r_c2, cs_r2 = st2[i]
                    p.wait("v", "s", cs_r2)
                    cv_ln3.append(p.emit("v", lambda e, scr=scrs[i], m_c2=m_c2, r_c2=r_c2:
                                         e.tensor_scalar(scr, scr, st(m_c2), st(r_c2),
                                                         ALU.subtract, ALU.mult)))
                cg_w = []
                for i, t in enumerate(ts):
                    p.wait("g", "v", cv_ln3[i])
                    p.wait("g", cb_lnw)
                    cg_w.append(p.emit("g", lambda e, scr=scrs[i]: e.tensor_mul(scr, scr, lnw[:])))
                    scr_rd_g[t % 2] = cg_w[i]
                cv_aff = []
                for i, t in enumerate(ts):
                    p.wait("v", "g", cg_w[i])
                    p.wait("v", cb_lnb)
                    p.wait("v", "t", lnbf_rd_t[t % 2])
                    cv_aff.append(p.emit("v", lambda e, lb=lbs[i], scr=scrs[i]:
                                         e.tensor_add(lb, scr, lnb[:])))
                for i, t in enumerate(ts):
                    last_t = 0
                    rdy = []
                    p.wait("t", "t", p.cnt["t"])
                    for half in range(2):
                        bank = tps[half]
                        eng_r, cnt_r = bank_rd[half]
                        p.wait("t", "v", cv_aff[i])
                        p.wait("t", eng_r, cnt_r)
                        p.wait("t", c_const)
                        for k8 in range(8):
                            kc = half * 8 + k8
                            def mktr(kc=kc, k8=k8, bank=bank, lb=lbs[i]):
                                return lambda e: e.transpose(
                                    bank[:, k8 * 128:(k8 + 1) * 128],
                                    lb[:, kc * 128:(kc + 1) * 128], ident[:])
                            last_t = p.emit("t", mktr())
                        eng = "v" if half == 0 else "s"
                        p.wait(eng, "t", last_t)
                        def mkev(half=half, t=t, bank=bank):
                            srcv = bank[:].rearrange("p (k c) -> p k c", c=128)
                            dst = xnT[:].rearrange("p (k c) -> p k c", c=1024)[
                                :, half * 8:(half + 1) * 8, 128 * t: 128 * t + 128]
                            return lambda e: _copy(e, dst, srcv)
                        c = p.emit(eng, mkev())
                        bank_rd[half] = (eng, c)
                        rdy.append((eng, c))
                    lnbf_rd_t[t % 2] = last_t
                    xnT_rdy[t] = rdy
                emit_ffn_mms(0, ts)
            e1_t_done = p.cnt["t"]
            e1_v_done = p.cnt["v"]
            e1_s_done = p.cnt["s"]

            for ng in range(1, 3):
                emit_ffn_stage(ng)
                emit_ffn_mms(ng, list(range(NT)))
            emit_ffn_stage(3)

            p.wait("v", "v", estate["amax_v"])
            cv_col = p.emit("v", lambda e: e.reduce_max(st(7), amaxc[:, 0:8], axis=AX.X))
            p.wait("g", "v", cv_col)
            cg_red = p.emit("g", lambda e: e.tensor_reduce(
                srow[0:1, :], st(7), axis=AX.C, op=ALU.max))
            p.wait("q", "g", cg_red)
            cq_s1 = p.dma("q", lambda e: e.dma_start(out=sc_scr[None, :], in_=srow[:]))
            p.wait("q", cq_s1)
            cq_s2 = p.dma("q", lambda e: e.dma_start(
                out=scalev[:], in_=sc_scr[None, :].to_broadcast((128, 1))))
            p.wait("v", cq_s2)
            p.wait("v", cb_gam)
            cv_gs = p.emit("v", lambda e: e.tensor_scalar(
                gam[:], gam[:], scalev[:], None, ALU.mult))

            ybuf_rd2 = [0, 0]
            scr_st_gd = [0, 0]
            ffn_done_pair = {}
            for tpair in range(0, NT, 2):
                emit_ffn_mms(3, [tpair, tpair + 1])
                ffn_done_pair[tpair] = (p.cnt["v"], p.cnt["s"])
            for tpair in range(0, NT, 2):
                ts = [tpair, tpair + 1]
                ffn_v_done, ffn_s_done = ffn_done_pair[tpair]
                scrs, sqds, yas, cqs, cvg, cgb = [], [], [], [], [], []
                for t in ts:
                    par = t % 2
                    ya = ybuf[:, par * D:(par + 1) * D]
                    p.wait("q", "v", max(ybuf_rd2[par], e1_v_done))
                    cq = p.dma("q", lambda e, t=t, ya=ya: e.dma_start(
                        out=ya, in_=y_hbm[128 * t:128 * t + 128, :]))
                    scrs.append(scr_e[:, par * D:(par + 1) * D])
                    sqds.append(lnbf2[:, par * D:(par + 1) * D])
                    yas.append(ya)
                    cqs.append(cq)
                for i, t in enumerate(ts):
                    par = t % 2
                    p.wait("v", "v", max(ffn_v_done, cv_gs))
                    p.wait("v", "s", ffn_s_done)
                    p.wait("v", "g", scr_rd_g[par])
                    p.wait("v", scr_st_gd[par])
                    cvg.append(p.emit("v", lambda e, t=t, scr=scrs[i]: e.tensor_mul(
                        scr, ffnstage[:, t * D:(t + 1) * D], gam[:])))
                for i, t in enumerate(ts):
                    p.wait("g", "v", cvg[i])
                    p.wait("g", cb_bet)
                    cgb.append(p.emit("g", lambda e, scr=scrs[i]: e.tensor_add(scr, scr, bet[:])))
                    scr_rd_g[t % 2] = cgb[i]
                for i, t in enumerate(ts):
                    p.wait("s", "g", cgb[i])
                    p.wait("s", "t", e1_t_done)
                    p.wait("v", "g", cgb[i])
                st3 = ln_stats2([(scrs[i], ts[i] % 2, sqds[i]) for i in range(2)])
                for i, t in enumerate(ts):
                    par = t % 2
                    m_c, r_c, cs_r = st3[i]
                    p.wait("v", "s", cs_r)
                    c_c1 = 6 + par * 8
                    cv_c1 = p.emit("v", lambda e, m_c=m_c, r_c=r_c, c_c1=c_c1: e.tensor_mul(
                        st(c_c1), st(m_c), st(r_c)))
                    p.wait("v", cqs[i])
                    p.wait("v", "v", cv_c1)
                    def mkf1(scr=scrs[i], ya=yas[i], r_c=r_c):
                        return lambda e: e.scalar_tensor_tensor(
                            scr, scr, st(r_c), ya, ALU.mult, ALU.add)
                    cv_f1 = p.emit("v", mkf1())
                    ybuf_rd2[par] = cv_f1
                    p.wait("v", "v", cv_f1)
                    def mkf2(scr=scrs[i], c_c1=c_c1):
                        return lambda e: e.tensor_scalar(scr, scr, st(c_c1), None, ALU.subtract)
                    cv_f2 = p.emit("v", mkf2())
                    p.wait("g", "v", cv_f2)
                    out_store = p.dma("g", lambda e, t=t, scr=scrs[i]: e.dma_start(
                        out=out_hbm[128 * t:128 * t + 128, :], in_=scr))
                    scr_st_gd[par] = out_store

            n = p.dma_idx["gd"]
            for cls in range(GDPOOL):
                k = (n - cls + GDPOOL - 1) // GDPOOL
                if k:
                    p.ops["g"].append(("wait", ("gd", cls), 16 * k))
            p.realize(nc, sems)
        eph.close()

    semstack.close()
    stack.close()
    return nc


_NC_CACHE = {}


def _host_prep(L0_attn_x, L0_attn_qkv_w, L0_attn_proj_w, L0_attn_proj_b,
               L0_ffn_weight, L0_ffn_gamma, L0_ffn_beta, L0_ffn_ln_weight,
               L0_ffn_ln_bias):
    bf = ml_dtypes.bfloat16
    x = np.ascontiguousarray(np.asarray(L0_attn_x, dtype=np.float32)[0])
    qkv_w = np.asarray(L0_attn_qkv_w, dtype=np.float32)
    W2 = np.ascontiguousarray(qkv_w[:4096].T)
    qk_strips = np.ascontiguousarray(
        W2.reshape(16, 128, 32, 128).transpose(2, 1, 0, 3))
    v_w = np.ascontiguousarray(qkv_w[4096:].T)
    proj_wt = np.ascontiguousarray(np.asarray(L0_attn_proj_w, np.float32).T)
    ffn_wt = np.ascontiguousarray(np.asarray(L0_ffn_weight, np.float32).T)
    bias_bc = np.ascontiguousarray(np.stack([
        np.broadcast_to(np.asarray(L0_attn_proj_b, np.float32), (128, D)),
        np.broadcast_to(np.asarray(L0_ffn_gamma, np.float32), (128, D)),
        np.broadcast_to(np.asarray(L0_ffn_beta, np.float32), (128, D)),
    ]))
    lnwb_bc = np.ascontiguousarray(np.stack([
        np.broadcast_to(np.asarray(L0_ffn_ln_weight, np.float32), (128, D)),
        np.broadcast_to(np.asarray(L0_ffn_ln_bias, np.float32), (128, D)),
    ]).astype(bf))
    ident = np.eye(128, dtype=bf)

    xp = np.zeros((SEQ + 2 * HALO, D), np.float32)
    xp[HALO:HALO + SEQ] = x
    x_shards = [np.ascontiguousarray(xp[c * OWN: c * OWN + OWN + 2 * HALO])
                for c in range(NCORES)]

    mask_list = []
    pcol = np.arange(128)[:, None]
    jcol = np.arange(KWIN)[None, :]
    for c in range(NCORES):
        mm = np.zeros((128, SW), np.float32)
        mb = np.full((128, SW), NEG, np.float32)
        for b in range(NT):
            q = c * OWN + b * 128 + pcol
            k = c * OWN + b * 128 - 8 + jcol
            d = np.abs(q - k).astype(np.float32)
            inw = (k >= q - 8) & (k < q + 8) & (k >= 0) & (k < SEQ)
            mval = np.where(inw, (8.0 - d / 2.0) / 8.0, 0.0) * 0.125
            bval = np.where(inw, 0.0, NEG)
            for h2 in range(2):
                mm[:, (b * 2 + h2) * KWIN:(b * 2 + h2 + 1) * KWIN] = mval
                mb[:, (b * 2 + h2) * KWIN:(b * 2 + h2 + 1) * KWIN] = bval
        mask_list.append(np.ascontiguousarray(np.stack([mm, mb]).astype(bf)))

    shared = {
        "qk_strips": qk_strips, "v_w": v_w, "proj_wt": proj_wt,
        "ffn_wt": ffn_wt, "bias_bc": bias_bc, "lnwb_bc": lnwb_bc,
        "ident": ident,
    }
    in_maps = []
    for c in range(NCORES):
        m = dict(shared)
        m["x_shard"] = x_shards[c]
        m["masks"] = mask_list[c]
        in_maps.append(m)
    return in_maps


def kernel(**inputs):
    in_maps = _host_prep(**inputs)
    if "nc" not in _NC_CACHE:
        _NC_CACHE["nc"] = build_program()
    nc = _NC_CACHE["nc"]
    res = run_bass_kernel_spmd(nc, in_maps, core_ids=list(range(NCORES)))
    out = np.concatenate([np.asarray(res.results[c]["out"]) for c in range(NCORES)], axis=0)
    return out.reshape(1, SEQ, D).astype(np.float32)


# version 24
# speedup vs baseline: 1.1893x; 1.1893x over previous
import os
import numpy as np
import ml_dtypes
from contextlib import ExitStack

import concourse.bass as bass
import concourse.mybir as mybir
from concourse.bass_utils import run_bass_kernel_spmd

F32 = mybir.dt.float32
BF16 = mybir.dt.bfloat16
AF = mybir.ActivationFunctionType
ALU = mybir.AluOpType
AX = mybir.AxisListType

D = 2048
HEADS = 32
HD = 64
NCORES = 8
OWN = 1024
HALO = 8
NT = 8
TOK = 1152
KWIN = 144
TOKQ = 1056
SW = NT * 2 * KWIN
NEG = -1.0e30
EPS = 1e-5
SEQ = 8192
KPH = int(os.environ.get("KPH", "99"))
CPART = int(os.environ.get("KCP", "99"))

ENGS = ("t", "v", "s", "g", "q")
SEMKEYS = ENGS + ("gd",)



def _copy(e, dst, src):
    if hasattr(e, "tensor_copy"):
        return e.tensor_copy(dst, src)
    return e.copy(dst, src)


QPOOL = 8
GDPOOL = 4
DMA_POOLS = {"q": QPOOL, "gd": GDPOOL}


class Prog:
    def __init__(self):
        self.ops = {k: [] for k in ENGS}
        self.cnt = {k: 0 for k in ENGS}
        self.dma_idx = {"q": 0, "gd": 0}
        self.waited = {k: {} for k in ENGS}

    def emit(self, eng, fn, inc=1):
        self.cnt[eng] += inc
        self.ops[eng].append(("op", fn, inc, eng))
        return self.cnt[eng]

    def dma(self, eng, fn):
        queue = "gd" if eng == "g" else eng
        idx = self.dma_idx[queue]
        self.dma_idx[queue] = idx + 1
        pool = DMA_POOLS[queue]
        semkey = (queue, idx % pool)
        val = 16 * (idx // pool + 1)
        self.ops[eng].append(("op", fn, 16, semkey))
        return (semkey, val)

    def wait(self, eng, on, val=None):
        if isinstance(on, tuple) and val is None:
            on, val = on
        if isinstance(val, tuple):
            raise TypeError("token passed as val")
        if on is None or val is None or val <= 0:
            return
        w = self.waited[eng]
        if w.get(on, 0) >= val:
            return
        w[on] = val
        self.ops[eng].append(("wait", on, val))

    def barrier(self):
        snap = dict(self.cnt)
        dsnap = dict(self.dma_idx)
        for e in ENGS:
            for o in ENGS:
                if o != e:
                    self.wait(e, o, snap[o])
            for queue, pool in DMA_POOLS.items():
                n = dsnap[queue]
                for cls in range(pool):
                    k = (n - cls + pool - 1) // pool
                    self.wait(e, (queue, cls), 16 * k)

    def realize(self, nc, sems):
        ops = {k: list(v) for k, v in self.ops.items()}
        for k in ENGS:
            self.ops[k] = []

        with nc.Block() as blk:
            def runner(key):
                def run(engine):
                    for item in ops[key]:
                        if item[0] == "op":
                            _, fn, inc, semkey = item
                            fn(engine).then_inc(sems[semkey], inc)
                        else:
                            _, on, val = item
                            engine.wait_ge(sems[on], val)
                return run
            blk.tensor(runner("t"))
            blk.vector(runner("v"))
            blk.scalar(runner("s"))
            blk.gpsimd(runner("g"))
            blk.sync(runner("q"))


def build_program():
    nc = bass.Bass("TRN2", target_bir_lowering=False, debug=False)

    x_hbm = nc.dram_tensor("x_shard", [OWN + 2 * HALO, D], F32, kind="ExternalInput").ap()
    qk_strips = nc.dram_tensor("qk_strips", [32, 128, 16, 128], F32, kind="ExternalInput").ap()
    v_w = nc.dram_tensor("v_w", [D, D], F32, kind="ExternalInput").ap()
    proj_wt = nc.dram_tensor("proj_wt", [D, D], F32, kind="ExternalInput").ap()
    ffn_wt = nc.dram_tensor("ffn_wt", [D, D], F32, kind="ExternalInput").ap()
    bias_bc = nc.dram_tensor("bias_bc", [3, 128, D], F32, kind="ExternalInput").ap()
    lnwb_bc = nc.dram_tensor("lnwb_bc", [2, 128, D], BF16, kind="ExternalInput").ap()
    masks = nc.dram_tensor("masks", [2, 128, SW], BF16, kind="ExternalInput").ap()
    ident_in = nc.dram_tensor("ident", [128, 128], BF16, kind="ExternalInput").ap()
    y_hbm = nc.dram_tensor("y_scratch", [OWN, D], F32, kind="Internal").ap()
    sc_scr = nc.dram_tensor("sc_scr", [1], F32, kind="Internal").ap()
    out_hbm = nc.dram_tensor("out", [OWN, D], F32, kind="ExternalOutput").ap()

    p = Prog()
    stack = ExitStack()

    ident = stack.enter_context(nc.sbuf_tensor("identb", [128, 128], BF16))
    stats = stack.enter_context(nc.sbuf_tensor("stats", [128, 80], F32))
    scalev = stack.enter_context(nc.sbuf_tensor("scalev", [128, 1], F32))
    nhalf = stack.enter_context(nc.sbuf_tensor("nhalf", [128, 1], F32))
    accD = stack.enter_context(nc.sbuf_tensor("accD", [128, 32], F32))
    accE = stack.enter_context(nc.sbuf_tensor("accE", [128, 64], F32))
    accL = stack.enter_context(nc.sbuf_tensor("accL", [128, 2], F32))

    def st(col):
        return stats[:, col:col + 1]

    big = ExitStack()
    lnT = big.enter_context(nc.sbuf_tensor("lnT", [128, 16 * TOK], BF16))
    qkv_stack = ExitStack()
    QT = qkv_stack.enter_context(nc.sbuf_tensor("QT", [128, 16 * TOKQ], BF16))
    KT = qkv_stack.enter_context(nc.sbuf_tensor("KT", [128, 16 * TOKQ], BF16))
    VTOK = qkv_stack.enter_context(nc.sbuf_tensor("VTOK", [128, 9 * D], BF16))

    semstack = ExitStack()
    sems = {}
    for k in ENGS:
        sems[k] = semstack.enter_context(nc.semaphore(f"sem_{k}"))
    for queue, pool in DMA_POOLS.items():
        for cls in range(pool):
            sems[(queue, cls)] = semstack.enter_context(
                nc.semaphore(f"sem_{queue}{cls}"))
    if True:

        c_const = p.dma("q", lambda e: e.dma_start(out=ident[:], in_=ident_in))
        p.emit("g", lambda e: e.memset(
            lnT[:].rearrange("p (k c) -> p k c", c=TOK)[:, :, 1040:TOK], 0.0))
        p.emit("g", lambda e: e.memset(nhalf[:], -0.5))
        c_zero = p.cnt["g"]

        def ln_stats(xa, par, sq_dump, n=D):
            c_sum, c_sq, c_mean, c_t1, c_var, c_rstd = (par * 8 + i for i in range(6))
            cs1 = p.emit("s", lambda e: e.activation(sq_dump, xa, AF.Square, accum_out=st(c_sq)))
            cv1 = p.emit("v", lambda e: e.reduce_sum(st(c_sum), xa, axis=AX.X))
            p.wait("v", "v", cv1)
            cv2 = p.emit("v", lambda e: e.tensor_scalar(st(c_mean), st(c_sum), 1.0 / n, None, ALU.mult))
            p.wait("v", "v", cv2)
            cv3 = p.emit("v", lambda e: e.tensor_mul(st(c_t1), st(c_mean), st(c_mean)))
            p.wait("v", "s", cs1)
            cv4 = p.emit("v", lambda e: e.tensor_scalar(st(c_var), st(c_sq), 1.0 / n, None, ALU.mult))
            p.wait("v", "v", max(cv3, cv4))
            cv5 = p.emit("v", lambda e: e.tensor_scalar(
                st(c_var), st(c_var), st(c_t1), EPS, ALU.subtract, ALU.add))
            p.wait("s", "v", cv5)
            cs2 = p.emit("s", lambda e: e.activation(st(c_t1), st(c_var), AF.Ln))
            p.wait("s", "s", cs2)
            p.wait("s", "g", c_zero)
            cs3 = p.emit("s", lambda e: e.activation(st(c_rstd), st(c_t1), AF.Exp, scale=nhalf[:]))
            return c_mean, c_rstd, cs3

        def ln_stats2(specs, presums=None):
            cols = [tuple(par * 8 + i for i in range(6)) for (_, par, _) in specs]
            cs1 = []
            for (xa, par, sqd), cc in zip(specs, cols):
                cs1.append(p.emit("s", lambda e, xa=xa, sqd=sqd, cc=cc: e.activation(
                    sqd, xa, AF.Square, accum_out=st(cc[1]))))
            cv1 = []
            for i, ((xa, par, sqd), cc) in enumerate(zip(specs, cols)):
                if presums is not None:
                    ap, rdy = presums[i]
                    for eng_r, cnt_r in rdy:
                        p.wait("v", eng_r, cnt_r)
                    cv1.append(p.emit("v", lambda e, ap=ap, cc=cc: e.reduce_sum(
                        st(cc[0]), ap, axis=AX.X)))
                else:
                    cv1.append(p.emit("v", lambda e, xa=xa, cc=cc: e.reduce_sum(
                        st(cc[0]), xa, axis=AX.X)))
            cv2 = []
            for i, cc in enumerate(cols):
                p.wait("v", "v", cv1[i])
                cv2.append(p.emit("v", lambda e, cc=cc: e.tensor_scalar(
                    st(cc[2]), st(cc[0]), 1.0 / D, None, ALU.mult)))
            cv3 = []
            for i, cc in enumerate(cols):
                p.wait("v", "v", cv2[i])
                cv3.append(p.emit("v", lambda e, cc=cc: e.tensor_mul(
                    st(cc[3]), st(cc[2]), st(cc[2]))))
            cv4 = []
            for i, cc in enumerate(cols):
                p.wait("v", "s", cs1[i])
                cv4.append(p.emit("v", lambda e, cc=cc: e.tensor_scalar(
                    st(cc[4]), st(cc[1]), 1.0 / D, None, ALU.mult)))
            cv5 = []
            for i, cc in enumerate(cols):
                p.wait("v", "v", max(cv3[i], cv4[i]))
                cv5.append(p.emit("v", lambda e, cc=cc: e.tensor_scalar(
                    st(cc[4]), st(cc[4]), st(cc[3]), EPS, ALU.subtract, ALU.add)))
            cs2 = []
            for i, cc in enumerate(cols):
                p.wait("s", "v", cv5[i])
                cs2.append(p.emit("s", lambda e, cc=cc: e.activation(
                    st(cc[3]), st(cc[4]), AF.Ln)))
            out = []
            for i, cc in enumerate(cols):
                p.wait("s", "s", cs2[i])
                p.wait("s", "g", c_zero)
                cs3 = p.emit("s", lambda e, cc=cc: e.activation(
                    st(cc[5]), st(cc[3]), AF.Exp, scale=nhalf[:]))
                out.append((cc[2], cc[5], cs3))
            return out

        with (
            nc.sbuf_tensor("xbuf", [128, 3 * D], F32) as xbuf,
            nc.sbuf_tensor("lnbf", [128, 3 * D], BF16) as lnbf,
            nc.sbuf_tensor("sqdump", [128, 3 * D], F32) as sqdump,
            nc.psum_tensor([128, 1024], BF16) as tpa0,
            nc.psum_tensor([128, 1024], BF16) as tpa1,
        ):
            tps = [tpa0, tpa1]
            xbuf_rd_v = [0, 0]
            xbuf_rd_s = [0, 0]
            lnbf_rd_t = [0, 0]
            bank_rd = [("v", 0), ("v", 0)]

            def transpose_tile(src_bf, valid_slices, cv_apply, dstT, hook_extra_wait=None):
                last_t = 0
                for half in range(2):
                    bank = tps[half]
                    eng_r, cnt_r = bank_rd[half]
                    p.wait("t", "v", cv_apply)
                    p.wait("t", eng_r, cnt_r)
                    p.wait("t", c_const)
                    for k8 in range(8):
                        kc = half * 8 + k8
                        def mk(kc=kc, k8=k8, bank=bank, src_bf=src_bf):
                            return lambda e: e.transpose(
                                bank[:, k8 * 128:(k8 + 1) * 128],
                                src_bf[:, kc * 128:(kc + 1) * 128], ident[:])
                        last_t = p.emit("t", mk())
                    eng = "v" if half == 0 else "s"
                    p.wait(eng, "t", last_t)
                    p.wait(eng, "g", c_zero)
                    c = 0
                    for (so, do, wd) in valid_slices:
                        def mkevac(half=half, so=so, do=do, wd=wd, bank=bank, dstT=dstT):
                            src = bank[:].rearrange("p (k c) -> p k c", c=128)[:, :, so:so + wd]
                            dst = dstT[:].rearrange("p (k c) -> p k c", c=TOK)[
                                :, half * 8:(half + 1) * 8, do:do + wd]
                            return lambda e: _copy(e, dst, src)
                        c = p.emit(eng, mkevac())
                    bank_rd[half] = (eng, c)
                return last_t

            halo_g = p.emit("g", lambda e: e.memset(xbuf[:, 0:D], 0.0))
            p.wait("q", "g", halo_g)
            hq1 = p.dma("q", lambda e: e.dma_start(out=xbuf[0:8, 0:D], in_=x_hbm[0:8, :]))
            hq2 = p.dma("q", lambda e: e.dma_start(
                out=xbuf[64:72, 0:D], in_=x_hbm[OWN + HALO:OWN + 2 * HALO, :]))
            halo_q = [hq1, hq2]

            xa = xbuf[:, 0:D]
            for tok in halo_q:
                p.wait("s", tok)
                p.wait("v", tok)
            p.wait("s", "g", halo_g)
            p.wait("v", "g", halo_g)
            m_c, r_c, cs_r = ln_stats(xa, 0, sqdump[:, 0:D])
            p.wait("v", "s", cs_r)
            cv_apply = p.emit("v", lambda e, xa=xa, m_c=m_c, r_c=r_c: e.tensor_scalar(
                lnbf[:, 0:D], xa, st(m_c), st(r_c), ALU.subtract, ALU.mult))
            xbuf_rd_v[0] = cv_apply
            xbuf_rd_s[0] = p.cnt["s"]
            lt = transpose_tile(lnbf[:, 0:D], [(0, 0, 8), (64, 1032, 8)], cv_apply, lnT)
            lnbf_rd_t[0] = lt

            groups = [[0, 1, 2], [3, 4, 5], [6, 7]]
            xbuf_rd_v = [xbuf_rd_v[0], 0, 0]
            xbuf_rd_s = [xbuf_rd_s[0], 0, 0]
            lnbf_rd_t = [lnbf_rd_t[0], 0, 0]
            for bs in groups:
                xas, lbs, cqs, pars = [], [], [], []
                for idx, b in enumerate(bs):
                    par = b % 3
                    xa = xbuf[:, par * D:(par + 1) * D]
                    p.wait("q", "v", xbuf_rd_v[par])
                    p.wait("q", "s", xbuf_rd_s[par])
                    cq = p.dma("q", lambda e, b=b, xa=xa: e.dma_start(
                        out=xa, in_=x_hbm[HALO + 128 * b: HALO + 128 * b + 128, :]))
                    xas.append(xa)
                    lbs.append(lnbf[:, par * D:(par + 1) * D])
                    cqs.append(cq)
                    pars.append(par)
                for i in range(len(bs)):
                    p.wait("s", cqs[i])
                    p.wait("v", cqs[i])
                stA = ln_stats2([(xas[i], pars[i], sqdump[:, pars[i] * D:(pars[i] + 1) * D])
                                 for i in range(len(bs))])
                for i, b in enumerate(bs):
                    par = pars[i]
                    m_c, r_c, cs_r = stA[i]
                    p.wait("v", "t", lnbf_rd_t[par])
                    p.wait("v", "s", cs_r)
                    cv_apply = p.emit("v", lambda e, lb=lbs[i], xa=xas[i], m_c=m_c, r_c=r_c:
                                      e.tensor_scalar(lb, xa, st(m_c), st(r_c),
                                                      ALU.subtract, ALU.mult))
                    xbuf_rd_v[par] = cv_apply
                    xbuf_rd_s[par] = p.cnt["s"]
                    lt = transpose_tile(lbs[i], [(0, 8 + 128 * b, 128)], cv_apply, lnT)
                    lnbf_rd_t[par] = lt
            p.barrier()
            p.realize(nc, sems)
            if KPH <= 1:
                return nc

        with (
            nc.sbuf_tensor("wstage", [128, 2 * D], F32) as wstage,
            nc.sbuf_tensor("wbf", [128, 2 * D], BF16) as wbf,
            nc.psum_tensor([128, 512], F32) as qk0,
            nc.psum_tensor([128, 512], F32) as qk1,
            nc.psum_tensor([128, 512], F32) as qk2,
            nc.psum_tensor([128, 512], F32) as qk3,
            nc.psum_tensor([128, 512], F32) as qk4,
            nc.psum_tensor([128, 512], F32) as qk5,
        ):
            banksets = [[qk0, qk1, qk2], [qk3, qk4, qk5]]
            stage_rd_v = [0, 0]
            wbf_rd_t = [0, 0]
            bank_rd = [[("v", 0)] * 3, [("v", 0)] * 3]
            for m in range(32):
                par = m % 2
                is_q = m < 16
                ngroups = [(8, 512), (520, 512)] if is_q else [(0, 512), (512, 512), (1024, 16)]
                p.wait("q", "v", stage_rd_v[par])
                cq = p.dma("q", lambda e, m=m, par=par: e.dma_start(
                    out=wstage[:, par * D:(par + 1) * D],
                    in_=qk_strips[m].rearrange("p k c -> p (k c)")))
                p.wait("v", cq)
                p.wait("v", "t", wbf_rd_t[par])
                cv_cast = p.emit("v", lambda e, par=par: _copy(e, 
                    wbf[:, par * D:(par + 1) * D], wstage[:, par * D:(par + 1) * D]))
                stage_rd_v[par] = cv_cast
                p.wait("t", "v", cv_cast)
                banks = banksets[par]
                for gi in range(len(ngroups)):
                    eng_r, cnt_r = bank_rd[par][gi]
                    p.wait("t", eng_r, cnt_r)
                last_mm = 0
                for kc in range(16):
                    for gi, (c0, wd) in enumerate(ngroups):
                        def mk(kc=kc, gi=gi, c0=c0, wd=wd, par=par, banks=banks):
                            return lambda e: e.matmul(
                                banks[gi][:, 0:wd],
                                wbf[:, par * D + kc * 128: par * D + (kc + 1) * 128],
                                lnT[:, kc * TOK + c0: kc * TOK + c0 + wd],
                                start=(kc == 0), stop=(kc == 15))
                        last_mm = p.emit("t", mk())
                wbf_rd_t[par] = last_mm
                dstT = QT if is_q else KT
                mi = m if is_q else m - 16
                for gi, (c0, wd) in enumerate(ngroups):
                    eng = "v" if (gi % 2 == 0) else "s"
                    p.wait(eng, "t", last_mm)
                    if eng == "s":
                        p.wait(eng, "g", c_zero)
                    def mkevac(gi=gi, c0=c0, wd=wd, mi=mi, dstT=dstT, banks=banks):
                        return lambda e: _copy(e, 
                            dstT[:, mi * TOKQ + c0: mi * TOKQ + c0 + wd], banks[gi][:, 0:wd])
                    c = p.emit(eng, mkevac())
                    bank_rd[par][gi] = (eng, c)
            p.barrier()
            p.realize(nc, sems)
            if KPH <= 2:
                return nc

        with (
            nc.sbuf_tensor("vstage", [128, 2 * 8 * 512], F32) as vstage,
            nc.sbuf_tensor("vwbf", [128, 2 * 16 * 512], BF16) as vwbf,
            nc.psum_tensor([128, 512], F32) as vp0,
            nc.psum_tensor([128, 512], F32) as vp1,
        ):
            vbanks = [vp0, vp1]
            stage_rd_v = [0, 0]
            vwbf_rd_t = [0, 0]
            bank_rd = [("v", 0), ("v", 0)]
            NGV = 16 * 512
            NHV = 8 * 512
            for ng in range(4):
                par = ng % 2
                cast_last = 0
                p.wait("v", "t", vwbf_rd_t[par])
                for hh in range(2):
                    slot = (ng * 2 + hh) % 2
                    p.wait("q", "v", stage_rd_v[slot])
                    cq = p.dma("q", lambda e, ng=ng, slot=slot, hh=hh: e.dma_start(
                        out=vstage[:, slot * NHV:(slot + 1) * NHV],
                        in_=v_w.rearrange("(k pp) o -> pp k o", pp=128)[
                            :, hh * 8:(hh + 1) * 8, ng * 512:(ng + 1) * 512]))
                    p.wait("v", cq)
                    def mkc(par=par, hh=hh, slot=slot):
                        return lambda e: _copy(e,
                            vwbf[:, par * NGV + hh * NHV: par * NGV + (hh + 1) * NHV],
                            vstage[:, slot * NHV:(slot + 1) * NHV])
                    cast_last = p.emit("v", mkc())
                    stage_rd_v[slot] = cast_last
                p.wait("t", "v", cast_last)
                last_mm = 0
                for vt in range(9):
                    bpar = vt % 2
                    eng_r, cnt_r = bank_rd[bpar]
                    p.wait("t", eng_r, cnt_r)
                    for kc in range(16):
                        def mk(vt=vt, kc=kc, par=par, bpar=bpar):
                            return lambda e: e.matmul(
                                vbanks[bpar][:],
                                lnT[:, kc * TOK + 128 * vt: kc * TOK + 128 * vt + 128],
                                vwbf[:, par * NGV + kc * 512: par * NGV + (kc + 1) * 512],
                                start=(kc == 0), stop=(kc == 15))
                        last_mm = p.emit("t", mk())
                    eng = "v" if (vt % 2 == 0) else "s"
                    p.wait(eng, "t", last_mm)
                    if eng == "s":
                        p.wait(eng, "g", c_zero)
                    def mkevac(vt=vt, ng=ng, bpar=bpar):
                        return lambda e: _copy(e, 
                            VTOK[:, vt * D + ng * 512: vt * D + (ng + 1) * 512],
                            vbanks[bpar][:])
                    c = p.emit(eng, mkevac())
                    bank_rd[bpar] = (eng, c)
                vwbf_rd_t[par] = last_mm
            p.barrier()
            p.realize(nc, sems)
            if KPH <= 3:
                return nc

        with (
            nc.sbuf_tensor("maskm", [128, SW], BF16) as maskm,
            nc.sbuf_tensor("maskb", [128, SW], BF16) as maskb,
            nc.sbuf_tensor("tstrip", [128, 2 * SW], F32) as tstrip,
            nc.sbuf_tensor("pstrip", [128, 2 * SW], BF16) as pstrip,
            nc.sbuf_tensor("ptbuf", [128, 4 * 1024], BF16) as ptbuf,
            nc.psum_tensor([128, 512], F32) as sp0,
            nc.psum_tensor([128, 512], F32) as sp1,
            nc.psum_tensor([128, 512], F32) as sp2,
            nc.psum_tensor([128, 512], F32) as sp3,
            nc.psum_tensor([128, 1024], BF16) as tpc0,
            nc.psum_tensor([128, 1024], BF16) as tpc1,
            nc.psum_tensor([128, 512], F32) as oc0,
            nc.psum_tensor([128, 512], F32) as oc1,
        ):
            if CPART >= 1:
                cm1 = p.dma("g", lambda e: e.dma_start(out=maskm[:], in_=masks[0]))
                cm2 = p.dma("g", lambda e: e.dma_start(out=maskb[:], in_=masks[1]))

            sbanks = [sp0, sp1, sp2, sp3]
            tbanks = [tpc0, tpc1]
            obanks = [oc0, oc1]
            s_rd = [("v", 0)] * 4
            t_rd = [("v", 0), ("v", 0)]
            o_rd = [("v", 0), ("v", 0)]
            strip_rd = [{"t": 0, "v": 0, "s": 0, "g": 0} for _ in range(2)]
            pt_rd_t = [0, 0, 0, 0]

            def c_front_a(ph):
                spar = ph % 2
                t_ap = tstrip[:, spar * SW:(spar + 1) * SW]
                ev_last = 0
                for h2 in range(2):
                    p.wait("t", "t", p.cnt["t"])
                    lo, hi = h2 * 64, h2 * 64 + 64
                    for b in range(NT):
                        sb = b % 4
                        eng_r, cnt_r = s_rd[sb]
                        p.wait("t", eng_r, cnt_r)
                        def mkmm(ph=ph, b=b, lo=lo, hi=hi, sb=sb):
                            return lambda e: e.matmul(
                                sbanks[sb][:, 0:KWIN],
                                QT[lo:hi, ph * TOKQ + 8 + 128 * b: ph * TOKQ + 8 + 128 * b + 128],
                                KT[lo:hi, ph * TOKQ + 128 * b: ph * TOKQ + 128 * b + KWIN],
                                start=True, stop=True)
                        mm_last = p.emit("t", mkmm())
                        if CPART < 1:
                            continue
                        p.wait("v", "t", mm_last)
                        p.wait("v", cm1)
                        p.wait("v", cm2)
                        p.wait("v", "g", strip_rd[spar]["g"])
                        p.wait("v", "s", strip_rd[spar]["s"])
                        def mkem(b=b, h2=h2, sb=sb, t_ap=t_ap):
                            off = (b * 2 + h2) * KWIN
                            return lambda e: e.tensor_mul(
                                t_ap[:, off:off + KWIN],
                                sbanks[sb][:, 0:KWIN],
                                maskm[:, off:off + KWIN])
                        ev_last = p.emit("v", mkem())
                        s_rd[sb] = ("v", ev_last)
                        if h2 == 1:
                            p.wait("g", "v", ev_last)
                            def mkbias(b=b, t_ap=t_ap):
                                off = b * 2 * KWIN
                                return lambda e: e.tensor_add(
                                    t_ap[:, off:off + 2 * KWIN],
                                    t_ap[:, off:off + 2 * KWIN],
                                    maskb[:, off:off + 2 * KWIN])
                            cg_bias = p.emit("g", mkbias())
                if CPART < 2:
                    return None
                strip_rd[spar]["g"] = cg_bias
                return cg_bias

            def c_front_b(ph, cg_bias):
                spar = ph % 2
                t_ap = tstrip[:, spar * SW:(spar + 1) * SW]
                p_ap = pstrip[:, spar * SW:(spar + 1) * SW]
                negmx = stats[:, 16 + 16 * spar: 32 + 16 * spar]
                rccol = stats[:, 48 + 16 * spar: 64 + 16 * spar]
                if cg_bias is None:
                    return None
                p.wait("v", "g", cg_bias)
                cv_mx = p.emit("v", lambda e, t_ap=t_ap, negmx=negmx: e.reduce_max(
                    negmx, t_ap.rearrange("p (s c) -> p s c", c=KWIN),
                    axis=AX.X, negate=True))
                p.wait("g", "v", cv_mx)
                cg_sub = p.emit("g", lambda e, t_ap=t_ap, negmx=negmx: e.tensor_add(
                    t_ap.rearrange("p (s c) -> p s c", c=KWIN),
                    t_ap.rearrange("p (s c) -> p s c", c=KWIN),
                    negmx[:, :, None].broadcast_to([128, 16, KWIN])))
                strip_rd[spar]["g"] = max(strip_rd[spar]["g"], cg_sub)
                p.wait("s", "g", cg_sub)
                cs_exp = p.emit("s", lambda e, t_ap=t_ap: e.activation(t_ap, t_ap, AF.Exp))
                strip_rd[spar]["s"] = cs_exp
                p.wait("v", "s", cs_exp)
                cv_sum = p.emit("v", lambda e, t_ap=t_ap, negmx=negmx: e.reduce_sum(
                    negmx, t_ap.rearrange("p (s c) -> p s c", c=KWIN), axis=AX.X))
                p.wait("v", "v", cv_sum)
                cv_rc = p.emit("v", lambda e, negmx=negmx, rccol=rccol: e.reciprocal(rccol, negmx))
                p.wait("g", "v", cv_rc)
                p.wait("g", "t", strip_rd[spar]["t"])
                cg_norm = p.emit("g", lambda e, t_ap=t_ap, p_ap=p_ap, rccol=rccol: e.tensor_mul(
                    p_ap.rearrange("p (s c) -> p s c", c=KWIN),
                    t_ap.rearrange("p (s c) -> p s c", c=KWIN),
                    rccol[:, :, None].broadcast_to([128, 16, KWIN])))
                strip_rd[spar]["g"] = max(strip_rd[spar]["g"], cg_norm)
                return cg_norm

            def c_back(ph, cv_norm):
                spar = ph % 2
                p_ap = pstrip[:, spar * SW:(spar + 1) * SW]
                if CPART < 3 or cv_norm is None:
                    return
                units = [(h2, b) for h2 in range(2) for b in range(NT)]
                c_ptes = [0, 0, 0, 0]
                p.wait("t", "g", cv_norm)
                p.wait("t", "t", p.cnt["t"])
                for g4 in range(4):
                    tb = g4 % 2
                    eng_r, cnt_r = t_rd[tb]
                    p.wait("t", eng_r, cnt_r)
                    tr_last = 0
                    for ui in range(4):
                        h2, b = units[g4 * 4 + ui]
                        off = (b * 2 + h2) * KWIN
                        def mktr1(off=off, ui=ui, tb=tb, p_ap=p_ap):
                            return lambda e: e.transpose(
                                tbanks[tb][:, ui * 128:(ui + 1) * 128],
                                p_ap[:, off:off + 128], ident[:])
                        p.emit("t", mktr1())
                        def mktr2(off=off, ui=ui, tb=tb, p_ap=p_ap):
                            return lambda e: e.transpose(
                                tbanks[tb][0:16, 512 + ui * 128: 512 + (ui + 1) * 128],
                                p_ap[:, off + 128:off + KWIN], ident[:])
                        tr_last = p.emit("t", mktr2())
                    strip_rd[spar]["t"] = tr_last
                    p.wait("s", "t", tr_last)
                    p.wait("s", "t", pt_rd_t[g4])
                    def mkpte1(tb=tb, g4=g4):
                        return lambda e: _copy(e,
                            ptbuf[:, g4 * 1024: g4 * 1024 + 512],
                            tbanks[tb][:, 0:512])
                    p.emit("s", mkpte1())
                    def mkpte2(tb=tb, g4=g4):
                        return lambda e: _copy(e,
                            ptbuf[0:16, g4 * 1024 + 512: g4 * 1024 + 1024],
                            tbanks[tb][0:16, 512:1024])
                    c_ptes[g4] = p.emit("s", mkpte2())
                    t_rd[tb] = ("s", c_ptes[g4])
                if CPART < 4:
                    return
                for gi, g4 in enumerate((0, 1, 2, 3)):
                    p.wait("t", "s", c_ptes[g4])
                    if gi in (0, 2):
                        p.wait("t", "t", p.cnt["t"])
                    om_last = 0
                    for ui in range(4):
                        h2, b = units[g4 * 4 + ui]
                        head = 2 * ph + h2
                        ob = b // 4
                        eng_r2, cnt_r2 = o_rd[ob]
                        p.wait("t", eng_r2, cnt_r2)
                        def mko1(ui=ui, h2=h2, b=b, head=head, ob=ob, g4=g4):
                            return lambda e: e.matmul(
                                obanks[ob][h2 * 64:(h2 + 1) * 64, (b % 4) * 128:(b % 4) * 128 + 128],
                                VTOK[:, b * D + head * HD: b * D + head * HD + HD],
                                ptbuf[:, g4 * 1024 + ui * 128: g4 * 1024 + (ui + 1) * 128],
                                start=True, stop=False)
                        p.emit("t", mko1())
                        def mko2(ui=ui, h2=h2, b=b, head=head, ob=ob, g4=g4):
                            return lambda e: e.matmul(
                                obanks[ob][h2 * 64:(h2 + 1) * 64, (b % 4) * 128:(b % 4) * 128 + 128],
                                VTOK[0:16, (b + 1) * D + head * HD: (b + 1) * D + head * HD + HD],
                                ptbuf[0:16, g4 * 1024 + 512 + ui * 128: g4 * 1024 + 512 + (ui + 1) * 128],
                                start=False, stop=True)
                        om_last = p.emit("t", mko2())
                    pt_rd_t[g4] = om_last
                    if CPART >= 5 and g4 >= 2:
                        ob = g4 - 2
                        eng = "s"
                        p.wait(eng, "t", om_last)
                        def mkoe(ob=ob, ph=ph):
                            return lambda e: _copy(e, 
                                lnT[:, ph * TOK + 8 + ob * 512: ph * TOK + 8 + ob * 512 + 512],
                                obanks[ob][:])
                        c_oe = p.emit(eng, mkoe())
                        o_rd[ob] = (eng, c_oe)

            cev = {}
            cnorm = {}
            for k in range(18):
                if k >= 2:
                    c_back(k - 2, cnorm[k - 2])
                if 1 <= k <= 16:
                    cnorm[k - 1] = c_front_b(k - 1, cev[k - 1])
                if k < 16:
                    cev[k] = c_front_a(k)
            p.barrier()
            p.realize(nc, sems)
            if KPH <= 4:
                return nc

        qkv_stack.close()

        with (
            nc.sbuf_tensor("x1", [128, NT * D], F32) as x1,
            nc.sbuf_tensor("projb", [128, D], F32) as projb,
            nc.sbuf_tensor("pwstage", [128, 2 * 8 * 512], F32) as pwstage,
            nc.sbuf_tensor("pwbf", [128, 2 * 16 * 512], BF16) as pwbf,
            nc.sbuf_tensor("xre", [128, 3 * D], F32) as xre,
            nc.sbuf_tensor("sqdump2", [128, D], F32) as sqdump2,
            nc.psum_tensor([128, 512], F32) as dp0,
            nc.psum_tensor([128, 512], F32) as dp1,
        ):
            cb = p.dma("g", lambda e: e.dma_start(out=projb[:], in_=bias_bc[0]))
            projb_q = cb
            dbanks = [dp0, dp1]
            NGP = 16 * 512
            NHS = 8 * 512
            stage_rd_v = [0, 0]
            pwbf_rd_t = [0, 0]
            bank_rd = [("v", 0), ("v", 0)]
            for ng in range(4):
                par = ng % 2
                cast_last = 0
                p.wait("v", "t", pwbf_rd_t[par])
                for hh in range(2):
                    slot = (ng * 2 + hh) % 2
                    p.wait("q", "v", stage_rd_v[slot])
                    cq = p.dma("q", lambda e, ng=ng, slot=slot, hh=hh: e.dma_start(
                        out=pwstage[:, slot * NHS:(slot + 1) * NHS],
                        in_=proj_wt.rearrange("(k pp) o -> pp k o", pp=128)[
                            :, hh * 8:(hh + 1) * 8, ng * 512:(ng + 1) * 512]))
                    p.wait("v", cq)
                    def mkc(par=par, hh=hh, slot=slot):
                        return lambda e: _copy(e, 
                            pwbf[:, par * NGP + hh * NHS: par * NGP + (hh + 1) * NHS],
                            pwstage[:, slot * NHS:(slot + 1) * NHS])
                    cast_last = p.emit("v", mkc())
                    stage_rd_v[slot] = cast_last
                p.wait("t", "v", cast_last)
                mm_last = 0
                for t in range(NT):
                    bpar = t % 2
                    eng_r, cnt_r = bank_rd[bpar]
                    p.wait("t", eng_r, cnt_r)
                    for kc in range(16):
                        def mk(t=t, kc=kc, par=par, bpar=bpar):
                            return lambda e: e.matmul(
                                dbanks[bpar][:],
                                lnT[:, kc * TOK + 8 + 128 * t: kc * TOK + 8 + 128 * t + 128],
                                pwbf[:, par * NGP + kc * 512: par * NGP + (kc + 1) * 512],
                                start=(kc == 0), stop=(kc == 15))
                        mm_last = p.emit("t", mk())
                    p.wait("v", "t", mm_last)
                    p.wait("v", projb_q)
                    def mkev(t=t, ng=ng, bpar=bpar):
                        return lambda e: e.scalar_tensor_tensor(
                            x1[:, t * D + ng * 512: t * D + (ng + 1) * 512],
                            dbanks[bpar][:], 1.0,
                            projb[:, ng * 512:(ng + 1) * 512],
                            ALU.mult, ALU.add,
                            accum_out=accD[:, t * 4 + ng: t * 4 + ng + 1])
                    c = p.emit("v", mkev())
                    bank_rd[bpar] = ("v", c)
                pwbf_rd_t[par] = mm_last
            proj_done_v = p.cnt["v"]

            xre_rd = [0, 0, 0]
            for bs in [[0, 1, 2], [3, 4, 5], [6, 7]]:
                ts = bs
                xas, xres, cqs = [], [], []
                for t in ts:
                    par = t % 3
                    p.wait("q", "v", xre_rd[par])
                    cq = p.dma("q", lambda e, t=t, par=par: e.dma_start(
                        out=xre[:, par * D:(par + 1) * D],
                        in_=x_hbm[HALO + 128 * t: HALO + 128 * t + 128, :]))
                    xas.append(x1[:, t * D:(t + 1) * D])
                    xres.append(xre[:, par * D:(par + 1) * D])
                    cqs.append(cq)
                p.wait("v", "v", proj_done_v)
                p.wait("s", "v", proj_done_v)
                dumps = [sqdump2[:, 0:D], pwstage[:, 0:D], pwstage[:, D:2 * D]]
                st4 = ln_stats2([(xas[i], ts[i] % 3, dumps[ts[i] % 3]) for i in range(len(ts))],
                                presums=[(accD[:, t * 4:(t + 1) * 4], []) for t in ts])
                for i, t in enumerate(ts):
                    par = t % 3
                    m_c, r_c, cs_r = st4[i]
                    p.wait("v", "s", cs_r)
                    c_c1 = 6 + par * 8
                    cv_c1 = p.emit("v", lambda e, m_c=m_c, r_c=r_c, c_c1=c_c1: e.tensor_mul(
                        st(c_c1), st(m_c), st(r_c)))
                    p.wait("v", cqs[i])
                    p.wait("v", "v", cv_c1)
                    def mke1(xa=xas[i], xr=xres[i], r_c=r_c):
                        return lambda e: e.scalar_tensor_tensor(
                            xa, xa, st(r_c), xr, ALU.mult, ALU.add)
                    cv_e1 = p.emit("v", mke1())
                    p.wait("v", "v", cv_e1)
                    def mke2(xa=xas[i], c_c1=c_c1):
                        return lambda e: e.tensor_scalar(xa, xa, st(c_c1), None, ALU.subtract)
                    cv_e2 = p.emit("v", mke2())
                    xre_rd[par] = cv_e2
                    p.wait("g", "v", cv_e2)
                    p.dma("g", lambda e, t=t, xa=xas[i]: e.dma_start(
                        out=y_hbm[128 * t:128 * t + 128, :], in_=xa))
            p.barrier()
            p.realize(nc, sems)
            if KPH <= 5:
                return nc

        big.close()

        eph = ExitStack()
        ybuf = eph.enter_context(nc.sbuf_tensor("ybuf", [128, 2 * D], F32))
        scr_e = eph.enter_context(nc.sbuf_tensor("scr_e", [128, 2 * D], F32))
        lnbf2 = eph.enter_context(nc.sbuf_tensor("lnbf2", [128, 2 * D], BF16))
        xnT = eph.enter_context(nc.sbuf_tensor("xnT", [128, 16 * 1024], BF16))
        ffnstage = eph.enter_context(nc.sbuf_tensor("ffnstage", [128, NT * D], BF16))
        fwstage = eph.enter_context(nc.sbuf_tensor("fwstage", [128, 2 * 8 * 512], F32))
        fwbf = eph.enter_context(nc.sbuf_tensor("fwbf", [128, 2 * 16 * 512], BF16))
        gam = eph.enter_context(nc.sbuf_tensor("gam", [128, D], F32))
        bet = eph.enter_context(nc.sbuf_tensor("bet", [128, D], F32))
        lnw = eph.enter_context(nc.sbuf_tensor("lnw", [128, D], BF16))
        lnb = eph.enter_context(nc.sbuf_tensor("lnb", [128, D], BF16))
        amaxc = eph.enter_context(nc.sbuf_tensor("amaxc", [128, 16], F32))
        srow = eph.enter_context(nc.sbuf_tensor("srow", [1, 1], F32))
        tpe0 = eph.enter_context(nc.psum_tensor([128, 1024], BF16))
        tpe1 = eph.enter_context(nc.psum_tensor([128, 1024], BF16))
        ep0 = eph.enter_context(nc.psum_tensor([128, 512], F32))
        ep1 = eph.enter_context(nc.psum_tensor([128, 512], F32))
        if True:
            cb_gam = p.dma("g", lambda e: e.dma_start(out=gam[:], in_=bias_bc[1]))
            cb_bet = p.dma("g", lambda e: e.dma_start(out=bet[:], in_=bias_bc[2]))
            cb_lnw = p.dma("g", lambda e: e.dma_start(out=lnw[:], in_=lnwb_bc[0]))
            cb_lnb = p.dma("g", lambda e: e.dma_start(out=lnb[:], in_=lnwb_bc[1]))
            vec_toks = [cb_gam, cb_bet, cb_lnw, cb_lnb]

            xnT_rdy = {}
            NGF = 16 * 512
            NHF = 8 * 512
            stage_rd_s = [0, 0]
            stage_rd_v = [0, 0]
            fwbf_rd_t = [0, 0]
            ebanks = [ep0, ep1]
            ebank_rd = [("v", 0), ("v", 0)]
            estate = {"amax_idx": 0, "slot": 0, "sign_last": {}}
            ffn_rdy = {}

            def emit_ffn_stage(ng):
                par = ng % 2
                sgn_last = 0
                p.wait("s", "t", fwbf_rd_t[par])
                for hh in range(2):
                    slot = estate["slot"] % 2
                    estate["slot"] += 1
                    p.wait("q", "s", stage_rd_s[slot])
                    p.wait("q", "v", stage_rd_v[slot])
                    cq = p.dma("q", lambda e, ng=ng, slot=slot, hh=hh: e.dma_start(
                        out=fwstage[:, slot * NHF:(slot + 1) * NHF],
                        in_=ffn_wt.rearrange("(k pp) o -> pp k o", pp=128)[
                            :, hh * 8:(hh + 1) * 8, ng * 512:(ng + 1) * 512]))
                    p.wait("s", cq)
                    def mks(par=par, hh=hh, slot=slot):
                        return lambda e: e.activation(
                            fwbf[:, par * NGF + hh * NHF: par * NGF + (hh + 1) * NHF],
                            fwstage[:, slot * NHF:(slot + 1) * NHF], AF.Sign)
                    sgn_last = p.emit("s", mks())
                    stage_rd_s[slot] = sgn_last
                    p.wait("v", cq)
                    idx = estate["amax_idx"]
                    estate["amax_idx"] += 1
                    cvm = p.emit("v", lambda e, slot=slot, idx=idx: e.reduce_max(
                        amaxc[:, idx:idx + 1],
                        fwstage[:, slot * NHF:(slot + 1) * NHF],
                        axis=AX.X, apply_absolute_value=True))
                    stage_rd_v[slot] = cvm
                    estate["amax_v"] = cvm
                estate["sign_last"][ng] = sgn_last

            def emit_ffn_mms(ng, tlist):
                par = ng % 2
                p.wait("t", "s", estate["sign_last"][ng])
                p.wait("t", "t", p.cnt["t"])
                mm_last = 0
                for t in tlist:
                    bpar = t % 2
                    eng_r, cnt_r = ebank_rd[bpar]
                    p.wait("t", eng_r, cnt_r)
                    for (eng_x, c_x) in xnT_rdy[t]:
                        p.wait("t", eng_x, c_x)
                    for kc in range(16):
                        def mk(t=t, kc=kc, ng=ng, par=par, bpar=bpar):
                            return lambda e: e.matmul(
                                ebanks[bpar][:],
                                xnT[:, kc * 1024 + 128 * t: kc * 1024 + 128 * t + 128],
                                fwbf[:, par * NGF + kc * 512: par * NGF + (kc + 1) * 512],
                                start=(kc == 0), stop=(kc == 15))
                        mm_last = p.emit("t", mk())
                    eng = "v" if (t % 2 == 0) else "s"
                    p.wait(eng, "t", mm_last)
                    def mkev(t=t, ng=ng, bpar=bpar):
                        return lambda e: _copy(e,
                            ffnstage[:, t * D + ng * 512: t * D + (ng + 1) * 512],
                            ebanks[bpar][:])
                    c = p.emit(eng, mkev())
                    ebank_rd[bpar] = (eng, c)
                    ffn_rdy.setdefault(t, []).append((eng, c))
                fwbf_rd_t[par] = max(fwbf_rd_t[par], mm_last)

            emit_ffn_stage(0)

            tps = [tpe0, tpe1]
            bank_rd = [("v", 0), ("v", 0)]
            lnbf_rd_t = [0, 0]
            scr_rd_g = [0, 0]
            ybuf_rd = [0, 0]
            for tpair in range(0, NT, 2):
                ts = [tpair, tpair + 1]
                yas, scrs, lbs, cqs = [], [], [], []
                for t in ts:
                    par = t % 2
                    ya = ybuf[:, par * D:(par + 1) * D]
                    p.wait("q", "v", ybuf_rd[par])
                    cq = p.dma("q", lambda e, t=t, ya=ya: e.dma_start(
                        out=ya, in_=y_hbm[128 * t:128 * t + 128, :]))
                    yas.append(ya)
                    scrs.append(scr_e[:, par * D:(par + 1) * D])
                    lbs.append(lnbf2[:, par * D:(par + 1) * D])
                    cqs.append(cq)
                for i, t in enumerate(ts):
                    p.wait("v", cqs[i])
                    p.wait("s", cqs[i])
                    p.wait("s", "t", lnbf_rd_t[t % 2])
                st1 = ln_stats2([(yas[i], ts[i] % 2, lbs[i]) for i in range(2)])
                cv_ln2 = []
                for i, t in enumerate(ts):
                    m_c, r_c, cs_r = st1[i]
                    p.wait("v", "s", cs_r)
                    p.wait("v", "g", scr_rd_g[t % 2])
                    cv_ln2.append(p.emit("v", lambda e, scr=scrs[i], ya=yas[i], m_c=m_c, r_c=r_c:
                                         e.tensor_scalar(scr, ya, st(m_c), st(r_c),
                                                         ALU.subtract, ALU.mult)))
                    ybuf_rd[t % 2] = cv_ln2[i]
                for i in range(2):
                    p.wait("s", "v", cv_ln2[i])
                    p.wait("v", "v", cv_ln2[i])
                st2 = ln_stats2([(scrs[i], ts[i] % 2, lbs[i]) for i in range(2)])
                cv_ln3 = []
                for i, t in enumerate(ts):
                    m_c2, r_c2, cs_r2 = st2[i]
                    p.wait("v", "s", cs_r2)
                    cv_ln3.append(p.emit("v", lambda e, scr=scrs[i], m_c2=m_c2, r_c2=r_c2:
                                         e.tensor_scalar(scr, scr, st(m_c2), st(r_c2),
                                                         ALU.subtract, ALU.mult)))
                cg_w = []
                for i, t in enumerate(ts):
                    p.wait("g", "v", cv_ln3[i])
                    p.wait("g", cb_lnw)
                    cg_w.append(p.emit("g", lambda e, scr=scrs[i]: e.tensor_mul(scr, scr, lnw[:])))
                    scr_rd_g[t % 2] = cg_w[i]
                cv_aff = []
                for i, t in enumerate(ts):
                    p.wait("v", "g", cg_w[i])
                    p.wait("v", cb_lnb)
                    p.wait("v", "t", lnbf_rd_t[t % 2])
                    cv_aff.append(p.emit("v", lambda e, lb=lbs[i], scr=scrs[i]:
                                         e.tensor_add(lb, scr, lnb[:])))
                for i, t in enumerate(ts):
                    last_t = 0
                    rdy = []
                    p.wait("t", "t", p.cnt["t"])
                    for half in range(2):
                        bank = tps[half]
                        eng_r, cnt_r = bank_rd[half]
                        p.wait("t", "v", cv_aff[i])
                        p.wait("t", eng_r, cnt_r)
                        p.wait("t", c_const)
                        for k8 in range(8):
                            kc = half * 8 + k8
                            def mktr(kc=kc, k8=k8, bank=bank, lb=lbs[i]):
                                return lambda e: e.transpose(
                                    bank[:, k8 * 128:(k8 + 1) * 128],
                                    lb[:, kc * 128:(kc + 1) * 128], ident[:])
                            last_t = p.emit("t", mktr())
                        eng = "v" if half == 0 else "s"
                        p.wait(eng, "t", last_t)
                        def mkev(half=half, t=t, bank=bank):
                            srcv = bank[:].rearrange("p (k c) -> p k c", c=128)
                            dst = xnT[:].rearrange("p (k c) -> p k c", c=1024)[
                                :, half * 8:(half + 1) * 8, 128 * t: 128 * t + 128]
                            return lambda e: _copy(e, dst, srcv)
                        c = p.emit(eng, mkev())
                        bank_rd[half] = (eng, c)
                        rdy.append((eng, c))
                    lnbf_rd_t[t % 2] = last_t
                    xnT_rdy[t] = rdy
                emit_ffn_mms(0, ts)
            e1_t_done = p.cnt["t"]
            e1_v_done = p.cnt["v"]
            e1_s_done = p.cnt["s"]

            for ng in range(1, 3):
                emit_ffn_stage(ng)
                emit_ffn_mms(ng, list(range(NT)))
            emit_ffn_stage(3)

            p.wait("v", "v", estate["amax_v"])
            cv_col = p.emit("v", lambda e: e.reduce_max(st(7), amaxc[:, 0:8], axis=AX.X))
            p.wait("g", "v", cv_col)
            cg_red = p.emit("g", lambda e: e.tensor_reduce(
                srow[0:1, :], st(7), axis=AX.C, op=ALU.max))
            p.wait("q", "g", cg_red)
            cq_s1 = p.dma("q", lambda e: e.dma_start(out=sc_scr[None, :], in_=srow[:]))
            p.wait("q", cq_s1)
            cq_s2 = p.dma("q", lambda e: e.dma_start(
                out=scalev[:], in_=sc_scr[None, :].to_broadcast((128, 1))))
            p.wait("v", cq_s2)
            p.wait("v", cb_gam)
            cv_gs = p.emit("v", lambda e: e.tensor_scalar(
                gam[:], gam[:], scalev[:], None, ALU.mult))

            ybuf_rd2 = [0, 0]
            scr_st_gd = [0, 0]
            ffn_done_pair = {}
            for tpair in range(0, NT, 2):
                emit_ffn_mms(3, [tpair, tpair + 1])
                ffn_done_pair[tpair] = (p.cnt["v"], p.cnt["s"])
            for tpair in range(0, NT, 2):
                ts = [tpair, tpair + 1]
                ffn_v_done, ffn_s_done = ffn_done_pair[tpair]
                scrs, sqds, yas, cqs, cvg, cgb = [], [], [], [], [], []
                for t in ts:
                    par = t % 2
                    ya = ybuf[:, par * D:(par + 1) * D]
                    p.wait("q", "v", max(ybuf_rd2[par], e1_v_done))
                    cq = p.dma("q", lambda e, t=t, ya=ya: e.dma_start(
                        out=ya, in_=y_hbm[128 * t:128 * t + 128, :]))
                    scrs.append(scr_e[:, par * D:(par + 1) * D])
                    sqds.append(lnbf2[:, par * D:(par + 1) * D])
                    yas.append(ya)
                    cqs.append(cq)
                for i, t in enumerate(ts):
                    par = t % 2
                    p.wait("v", "v", max(ffn_v_done, cv_gs))
                    p.wait("v", "s", ffn_s_done)
                    p.wait("v", "g", scr_rd_g[par])
                    p.wait("v", scr_st_gd[par])
                    cvg.append(p.emit("v", lambda e, t=t, scr=scrs[i]: e.tensor_mul(
                        scr, ffnstage[:, t * D:(t + 1) * D], gam[:])))
                for i, t in enumerate(ts):
                    p.wait("g", "v", cvg[i])
                    p.wait("g", cb_bet)
                    cgb.append(p.emit("g", lambda e, scr=scrs[i]: e.tensor_add(scr, scr, bet[:])))
                    scr_rd_g[t % 2] = cgb[i]
                for i, t in enumerate(ts):
                    p.wait("s", "g", cgb[i])
                    p.wait("s", "t", e1_t_done)
                    p.wait("v", "g", cgb[i])
                st3 = ln_stats2([(scrs[i], ts[i] % 2, sqds[i]) for i in range(2)])
                for i, t in enumerate(ts):
                    par = t % 2
                    m_c, r_c, cs_r = st3[i]
                    p.wait("v", "s", cs_r)
                    c_c1 = 6 + par * 8
                    cv_c1 = p.emit("v", lambda e, m_c=m_c, r_c=r_c, c_c1=c_c1: e.tensor_mul(
                        st(c_c1), st(m_c), st(r_c)))
                    p.wait("v", cqs[i])
                    p.wait("v", "v", cv_c1)
                    def mkf1(scr=scrs[i], ya=yas[i], r_c=r_c):
                        return lambda e: e.scalar_tensor_tensor(
                            scr, scr, st(r_c), ya, ALU.mult, ALU.add)
                    cv_f1 = p.emit("v", mkf1())
                    ybuf_rd2[par] = cv_f1
                    p.wait("v", "v", cv_f1)
                    def mkf2(scr=scrs[i], c_c1=c_c1):
                        return lambda e: e.tensor_scalar(scr, scr, st(c_c1), None, ALU.subtract)
                    cv_f2 = p.emit("v", mkf2())
                    p.wait("g", "v", cv_f2)
                    out_store = p.dma("g", lambda e, t=t, scr=scrs[i]: e.dma_start(
                        out=out_hbm[128 * t:128 * t + 128, :], in_=scr))
                    scr_st_gd[par] = out_store

            n = p.dma_idx["gd"]
            for cls in range(GDPOOL):
                k = (n - cls + GDPOOL - 1) // GDPOOL
                if k:
                    p.ops["g"].append(("wait", ("gd", cls), 16 * k))
            p.realize(nc, sems)
        eph.close()

    semstack.close()
    stack.close()
    return nc


_NC_CACHE = {}


def _host_prep(L0_attn_x, L0_attn_qkv_w, L0_attn_proj_w, L0_attn_proj_b,
               L0_ffn_weight, L0_ffn_gamma, L0_ffn_beta, L0_ffn_ln_weight,
               L0_ffn_ln_bias):
    bf = ml_dtypes.bfloat16
    x = np.ascontiguousarray(np.asarray(L0_attn_x, dtype=np.float32)[0])
    qkv_w = np.asarray(L0_attn_qkv_w, dtype=np.float32)
    W2 = np.ascontiguousarray(qkv_w[:4096].T)
    qk_strips = np.ascontiguousarray(
        W2.reshape(16, 128, 32, 128).transpose(2, 1, 0, 3))
    v_w = np.ascontiguousarray(qkv_w[4096:].T)
    proj_wt = np.ascontiguousarray(np.asarray(L0_attn_proj_w, np.float32).T)
    ffn_wt = np.ascontiguousarray(np.asarray(L0_ffn_weight, np.float32).T)
    bias_bc = np.ascontiguousarray(np.stack([
        np.broadcast_to(np.asarray(L0_attn_proj_b, np.float32), (128, D)),
        np.broadcast_to(np.asarray(L0_ffn_gamma, np.float32), (128, D)),
        np.broadcast_to(np.asarray(L0_ffn_beta, np.float32), (128, D)),
    ]))
    lnwb_bc = np.ascontiguousarray(np.stack([
        np.broadcast_to(np.asarray(L0_ffn_ln_weight, np.float32), (128, D)),
        np.broadcast_to(np.asarray(L0_ffn_ln_bias, np.float32), (128, D)),
    ]).astype(bf))
    ident = np.eye(128, dtype=bf)

    xp = np.zeros((SEQ + 2 * HALO, D), np.float32)
    xp[HALO:HALO + SEQ] = x
    x_shards = [np.ascontiguousarray(xp[c * OWN: c * OWN + OWN + 2 * HALO])
                for c in range(NCORES)]

    mask_list = []
    pcol = np.arange(128)[:, None]
    jcol = np.arange(KWIN)[None, :]
    for c in range(NCORES):
        mm = np.zeros((128, SW), np.float32)
        mb = np.full((128, SW), NEG, np.float32)
        for b in range(NT):
            q = c * OWN + b * 128 + pcol
            k = c * OWN + b * 128 - 8 + jcol
            d = np.abs(q - k).astype(np.float32)
            inw = (k >= q - 8) & (k < q + 8) & (k >= 0) & (k < SEQ)
            mval = np.where(inw, (8.0 - d / 2.0) / 8.0, 0.0) * 0.125
            bval = np.where(inw, 0.0, NEG)
            for h2 in range(2):
                mm[:, (b * 2 + h2) * KWIN:(b * 2 + h2 + 1) * KWIN] = mval
                mb[:, (b * 2 + h2) * KWIN:(b * 2 + h2 + 1) * KWIN] = bval
        mask_list.append(np.ascontiguousarray(np.stack([mm, mb]).astype(bf)))

    shared = {
        "qk_strips": qk_strips, "v_w": v_w, "proj_wt": proj_wt,
        "ffn_wt": ffn_wt, "bias_bc": bias_bc, "lnwb_bc": lnwb_bc,
        "ident": ident,
    }
    in_maps = []
    for c in range(NCORES):
        m = dict(shared)
        m["x_shard"] = x_shards[c]
        m["masks"] = mask_list[c]
        in_maps.append(m)
    return in_maps


def kernel(**inputs):
    in_maps = _host_prep(**inputs)
    if "nc" not in _NC_CACHE:
        _NC_CACHE["nc"] = build_program()
    nc = _NC_CACHE["nc"]
    res = run_bass_kernel_spmd(nc, in_maps, core_ids=list(range(NCORES)))
    out = np.concatenate([np.asarray(res.results[c]["out"]) for c in range(NCORES)], axis=0)
    return out.reshape(1, SEQ, D).astype(np.float32)


# version 25
# speedup vs baseline: 1.1938x; 1.0037x over previous
import os
import numpy as np
import ml_dtypes
from contextlib import ExitStack

import concourse.bass as bass
import concourse.mybir as mybir
from concourse.bass_utils import run_bass_kernel_spmd

F32 = mybir.dt.float32
BF16 = mybir.dt.bfloat16
AF = mybir.ActivationFunctionType
ALU = mybir.AluOpType
AX = mybir.AxisListType

D = 2048
HEADS = 32
HD = 64
NCORES = 8
OWN = 1024
HALO = 8
NT = 8
TOK = 1152
KWIN = 144
TOKQ = 1056
SW = NT * 2 * KWIN
NEG = -1.0e30
EPS = 1e-5
SEQ = 8192
KPH = int(os.environ.get("KPH", "99"))
CPART = int(os.environ.get("KCP", "99"))

ENGS = ("t", "v", "s", "g", "q")
SEMKEYS = ENGS + ("gd",)



def _copy(e, dst, src):
    if hasattr(e, "tensor_copy"):
        return e.tensor_copy(dst, src)
    return e.copy(dst, src)


QPOOL = 8
GDPOOL = 4
DMA_POOLS = {"q": QPOOL, "gd": GDPOOL}


class Prog:
    def __init__(self):
        self.ops = {k: [] for k in ENGS}
        self.cnt = {k: 0 for k in ENGS}
        self.dma_idx = {"q": 0, "gd": 0}
        self.waited = {k: {} for k in ENGS}

    def emit(self, eng, fn, inc=1):
        self.cnt[eng] += inc
        self.ops[eng].append(("op", fn, inc, eng))
        return self.cnt[eng]

    def dma(self, eng, fn):
        queue = "gd" if eng == "g" else eng
        idx = self.dma_idx[queue]
        self.dma_idx[queue] = idx + 1
        pool = DMA_POOLS[queue]
        semkey = (queue, idx % pool)
        val = 16 * (idx // pool + 1)
        self.ops[eng].append(("op", fn, 16, semkey))
        return (semkey, val)

    def wait(self, eng, on, val=None):
        if isinstance(on, tuple) and val is None:
            on, val = on
        if isinstance(val, tuple):
            raise TypeError("token passed as val")
        if on is None or val is None or val <= 0:
            return
        w = self.waited[eng]
        if w.get(on, 0) >= val:
            return
        w[on] = val
        self.ops[eng].append(("wait", on, val))

    def barrier(self):
        snap = dict(self.cnt)
        dsnap = dict(self.dma_idx)
        for e in ENGS:
            for o in ENGS:
                if o != e:
                    self.wait(e, o, snap[o])
            for queue, pool in DMA_POOLS.items():
                n = dsnap[queue]
                for cls in range(pool):
                    k = (n - cls + pool - 1) // pool
                    self.wait(e, (queue, cls), 16 * k)

    def realize(self, nc, sems):
        ops = {k: list(v) for k, v in self.ops.items()}
        for k in ENGS:
            self.ops[k] = []

        with nc.Block() as blk:
            def runner(key):
                def run(engine):
                    for item in ops[key]:
                        if item[0] == "op":
                            _, fn, inc, semkey = item
                            fn(engine).then_inc(sems[semkey], inc)
                        else:
                            _, on, val = item
                            engine.wait_ge(sems[on], val)
                return run
            blk.tensor(runner("t"))
            blk.vector(runner("v"))
            blk.scalar(runner("s"))
            blk.gpsimd(runner("g"))
            blk.sync(runner("q"))


def build_program():
    nc = bass.Bass("TRN2", target_bir_lowering=False, debug=False)

    x_hbm = nc.dram_tensor("x_shard", [OWN + 2 * HALO, D], F32, kind="ExternalInput").ap()
    qk_strips = nc.dram_tensor("qk_strips", [32, 128, 16, 128], F32, kind="ExternalInput").ap()
    v_w = nc.dram_tensor("v_w", [D, D], F32, kind="ExternalInput").ap()
    proj_wt = nc.dram_tensor("proj_wt", [D, D], F32, kind="ExternalInput").ap()
    ffn_wt = nc.dram_tensor("ffn_wt", [D, D], F32, kind="ExternalInput").ap()
    bias_bc = nc.dram_tensor("bias_bc", [3, 128, D], F32, kind="ExternalInput").ap()
    lnwb_bc = nc.dram_tensor("lnwb_bc", [2, 128, D], BF16, kind="ExternalInput").ap()
    masks = nc.dram_tensor("masks", [2, 128, SW], BF16, kind="ExternalInput").ap()
    ident_in = nc.dram_tensor("ident", [128, 128], BF16, kind="ExternalInput").ap()
    y_hbm = nc.dram_tensor("y_scratch", [OWN, D], F32, kind="Internal").ap()
    sc_scr = nc.dram_tensor("sc_scr", [1], F32, kind="Internal").ap()
    out_hbm = nc.dram_tensor("out", [OWN, D], F32, kind="ExternalOutput").ap()

    p = Prog()
    stack = ExitStack()

    ident = stack.enter_context(nc.sbuf_tensor("identb", [128, 128], BF16))
    stats = stack.enter_context(nc.sbuf_tensor("stats", [128, 80], F32))
    scalev = stack.enter_context(nc.sbuf_tensor("scalev", [128, 1], F32))
    nhalf = stack.enter_context(nc.sbuf_tensor("nhalf", [128, 1], F32))
    accD = stack.enter_context(nc.sbuf_tensor("accD", [128, 32], F32))
    accE = stack.enter_context(nc.sbuf_tensor("accE", [128, 64], F32))
    accL = stack.enter_context(nc.sbuf_tensor("accL", [128, 2], F32))

    def st(col):
        return stats[:, col:col + 1]

    big = ExitStack()
    lnT = big.enter_context(nc.sbuf_tensor("lnT", [128, 16 * TOK], BF16))
    qkv_stack = ExitStack()
    QT = qkv_stack.enter_context(nc.sbuf_tensor("QT", [128, 16 * TOKQ], BF16))
    KT = qkv_stack.enter_context(nc.sbuf_tensor("KT", [128, 16 * TOKQ], BF16))
    VTOK = qkv_stack.enter_context(nc.sbuf_tensor("VTOK", [128, 9 * D], BF16))

    semstack = ExitStack()
    sems = {}
    for k in ENGS:
        sems[k] = semstack.enter_context(nc.semaphore(f"sem_{k}"))
    for queue, pool in DMA_POOLS.items():
        for cls in range(pool):
            sems[(queue, cls)] = semstack.enter_context(
                nc.semaphore(f"sem_{queue}{cls}"))
    if True:

        c_const = p.dma("q", lambda e: e.dma_start(out=ident[:], in_=ident_in))
        p.emit("g", lambda e: e.memset(
            lnT[:].rearrange("p (k c) -> p k c", c=TOK)[:, :, 1040:TOK], 0.0))
        p.emit("g", lambda e: e.memset(nhalf[:], -0.5))
        c_zero = p.cnt["g"]

        def ln_stats(xa, par, sq_dump, n=D):
            c_sum, c_sq, c_mean, c_t1, c_var, c_rstd = (par * 8 + i for i in range(6))
            cs1 = p.emit("s", lambda e: e.activation(sq_dump, xa, AF.Square, accum_out=st(c_sq)))
            cv1 = p.emit("v", lambda e: e.reduce_sum(st(c_sum), xa, axis=AX.X))
            p.wait("v", "v", cv1)
            cv2 = p.emit("v", lambda e: e.tensor_scalar(st(c_mean), st(c_sum), 1.0 / n, None, ALU.mult))
            p.wait("v", "v", cv2)
            cv3 = p.emit("v", lambda e: e.tensor_mul(st(c_t1), st(c_mean), st(c_mean)))
            p.wait("v", "s", cs1)
            cv4 = p.emit("v", lambda e: e.tensor_scalar(st(c_var), st(c_sq), 1.0 / n, None, ALU.mult))
            p.wait("v", "v", max(cv3, cv4))
            cv5 = p.emit("v", lambda e: e.tensor_scalar(
                st(c_var), st(c_var), st(c_t1), EPS, ALU.subtract, ALU.add))
            p.wait("s", "v", cv5)
            cs2 = p.emit("s", lambda e: e.activation(st(c_t1), st(c_var), AF.Ln))
            p.wait("s", "s", cs2)
            p.wait("s", "g", c_zero)
            cs3 = p.emit("s", lambda e: e.activation(st(c_rstd), st(c_t1), AF.Exp, scale=nhalf[:]))
            return c_mean, c_rstd, cs3

        def ln_stats2(specs, presums=None):
            cols = [tuple(par * 8 + i for i in range(6)) for (_, par, _) in specs]
            cs1 = []
            for (xa, par, sqd), cc in zip(specs, cols):
                cs1.append(p.emit("s", lambda e, xa=xa, sqd=sqd, cc=cc: e.activation(
                    sqd, xa, AF.Square, accum_out=st(cc[1]))))
            cv1 = []
            for i, ((xa, par, sqd), cc) in enumerate(zip(specs, cols)):
                if presums is not None:
                    ap, rdy = presums[i]
                    for eng_r, cnt_r in rdy:
                        p.wait("v", eng_r, cnt_r)
                    cv1.append(p.emit("v", lambda e, ap=ap, cc=cc: e.reduce_sum(
                        st(cc[0]), ap, axis=AX.X)))
                else:
                    cv1.append(p.emit("v", lambda e, xa=xa, cc=cc: e.reduce_sum(
                        st(cc[0]), xa, axis=AX.X)))
            cv2 = []
            for i, cc in enumerate(cols):
                p.wait("v", "v", cv1[i])
                cv2.append(p.emit("v", lambda e, cc=cc: e.tensor_scalar(
                    st(cc[2]), st(cc[0]), 1.0 / D, None, ALU.mult)))
            cv3 = []
            for i, cc in enumerate(cols):
                p.wait("v", "v", cv2[i])
                cv3.append(p.emit("v", lambda e, cc=cc: e.tensor_mul(
                    st(cc[3]), st(cc[2]), st(cc[2]))))
            cv4 = []
            for i, cc in enumerate(cols):
                p.wait("v", "s", cs1[i])
                cv4.append(p.emit("v", lambda e, cc=cc: e.tensor_scalar(
                    st(cc[4]), st(cc[1]), 1.0 / D, None, ALU.mult)))
            cv5 = []
            for i, cc in enumerate(cols):
                p.wait("v", "v", max(cv3[i], cv4[i]))
                cv5.append(p.emit("v", lambda e, cc=cc: e.tensor_scalar(
                    st(cc[4]), st(cc[4]), st(cc[3]), EPS, ALU.subtract, ALU.add)))
            cs2 = []
            for i, cc in enumerate(cols):
                p.wait("s", "v", cv5[i])
                cs2.append(p.emit("s", lambda e, cc=cc: e.activation(
                    st(cc[3]), st(cc[4]), AF.Ln)))
            out = []
            for i, cc in enumerate(cols):
                p.wait("s", "s", cs2[i])
                p.wait("s", "g", c_zero)
                cs3 = p.emit("s", lambda e, cc=cc: e.activation(
                    st(cc[5]), st(cc[3]), AF.Exp, scale=nhalf[:]))
                out.append((cc[2], cc[5], cs3))
            return out

        with (
            nc.sbuf_tensor("xbuf", [128, 3 * D], F32) as xbuf,
            nc.sbuf_tensor("lnbf", [128, 3 * D], BF16) as lnbf,
            nc.sbuf_tensor("sqdump", [128, 3 * D], F32) as sqdump,
            nc.psum_tensor([128, 1024], BF16) as tpa0,
            nc.psum_tensor([128, 1024], BF16) as tpa1,
        ):
            tps = [tpa0, tpa1]
            xbuf_rd_v = [0, 0]
            xbuf_rd_s = [0, 0]
            lnbf_rd_t = [0, 0]
            bank_rd = [("v", 0), ("v", 0)]

            def transpose_tile(src_bf, valid_slices, cv_apply, dstT, hook_extra_wait=None):
                last_t = 0
                for half in range(2):
                    bank = tps[half]
                    eng_r, cnt_r = bank_rd[half]
                    p.wait("t", "v", cv_apply)
                    p.wait("t", eng_r, cnt_r)
                    p.wait("t", c_const)
                    for k8 in range(8):
                        kc = half * 8 + k8
                        def mk(kc=kc, k8=k8, bank=bank, src_bf=src_bf):
                            return lambda e: e.transpose(
                                bank[:, k8 * 128:(k8 + 1) * 128],
                                src_bf[:, kc * 128:(kc + 1) * 128], ident[:])
                        last_t = p.emit("t", mk())
                    eng = "v" if half == 0 else "s"
                    p.wait(eng, "t", last_t)
                    p.wait(eng, "g", c_zero)
                    c = 0
                    for (so, do, wd) in valid_slices:
                        def mkevac(half=half, so=so, do=do, wd=wd, bank=bank, dstT=dstT):
                            src = bank[:].rearrange("p (k c) -> p k c", c=128)[:, :, so:so + wd]
                            dst = dstT[:].rearrange("p (k c) -> p k c", c=TOK)[
                                :, half * 8:(half + 1) * 8, do:do + wd]
                            return lambda e: _copy(e, dst, src)
                        c = p.emit(eng, mkevac())
                    bank_rd[half] = (eng, c)
                return last_t

            halo_g = p.emit("g", lambda e: e.memset(xbuf[:, 0:D], 0.0))
            p.wait("q", "g", halo_g)
            hq1 = p.dma("q", lambda e: e.dma_start(out=xbuf[0:8, 0:D], in_=x_hbm[0:8, :]))
            hq2 = p.dma("q", lambda e: e.dma_start(
                out=xbuf[64:72, 0:D], in_=x_hbm[OWN + HALO:OWN + 2 * HALO, :]))
            halo_q = [hq1, hq2]

            xa = xbuf[:, 0:D]
            for tok in halo_q:
                p.wait("s", tok)
                p.wait("v", tok)
            p.wait("s", "g", halo_g)
            p.wait("v", "g", halo_g)
            m_c, r_c, cs_r = ln_stats(xa, 0, sqdump[:, 0:D])
            p.wait("v", "s", cs_r)
            cv_apply = p.emit("v", lambda e, xa=xa, m_c=m_c, r_c=r_c: e.tensor_scalar(
                lnbf[:, 0:D], xa, st(m_c), st(r_c), ALU.subtract, ALU.mult))
            xbuf_rd_v[0] = cv_apply
            xbuf_rd_s[0] = p.cnt["s"]
            lt = transpose_tile(lnbf[:, 0:D], [(0, 0, 8), (64, 1032, 8)], cv_apply, lnT)
            lnbf_rd_t[0] = lt

            groups = [[0, 1, 2], [3, 4, 5], [6, 7]]
            xbuf_rd_v = [xbuf_rd_v[0], 0, 0]
            xbuf_rd_s = [xbuf_rd_s[0], 0, 0]
            lnbf_rd_t = [lnbf_rd_t[0], 0, 0]
            for bs in groups:
                xas, lbs, cqs, pars = [], [], [], []
                for idx, b in enumerate(bs):
                    par = b % 3
                    xa = xbuf[:, par * D:(par + 1) * D]
                    p.wait("q", "v", xbuf_rd_v[par])
                    p.wait("q", "s", xbuf_rd_s[par])
                    cq = p.dma("q", lambda e, b=b, xa=xa: e.dma_start(
                        out=xa, in_=x_hbm[HALO + 128 * b: HALO + 128 * b + 128, :]))
                    xas.append(xa)
                    lbs.append(lnbf[:, par * D:(par + 1) * D])
                    cqs.append(cq)
                    pars.append(par)
                for i in range(len(bs)):
                    p.wait("s", cqs[i])
                    p.wait("v", cqs[i])
                stA = ln_stats2([(xas[i], pars[i], sqdump[:, pars[i] * D:(pars[i] + 1) * D])
                                 for i in range(len(bs))])
                for i, b in enumerate(bs):
                    par = pars[i]
                    m_c, r_c, cs_r = stA[i]
                    p.wait("v", "t", lnbf_rd_t[par])
                    p.wait("v", "s", cs_r)
                    cv_apply = p.emit("v", lambda e, lb=lbs[i], xa=xas[i], m_c=m_c, r_c=r_c:
                                      e.tensor_scalar(lb, xa, st(m_c), st(r_c),
                                                      ALU.subtract, ALU.mult))
                    xbuf_rd_v[par] = cv_apply
                    xbuf_rd_s[par] = p.cnt["s"]
                    lt = transpose_tile(lbs[i], [(0, 8 + 128 * b, 128)], cv_apply, lnT)
                    lnbf_rd_t[par] = lt
            p.barrier()
            p.realize(nc, sems)
            if KPH <= 1:
                return nc

        with (
            nc.sbuf_tensor("wstage", [128, 2 * D], F32) as wstage,
            nc.sbuf_tensor("wbf", [128, 2 * D], BF16) as wbf,
            nc.psum_tensor([128, 512], F32) as qk0,
            nc.psum_tensor([128, 512], F32) as qk1,
            nc.psum_tensor([128, 512], F32) as qk2,
            nc.psum_tensor([128, 512], F32) as qk3,
            nc.psum_tensor([128, 512], F32) as qk4,
            nc.psum_tensor([128, 512], F32) as qk5,
        ):
            banksets = [[qk0, qk1, qk2], [qk3, qk4, qk5]]
            stage_rd_v = [0, 0]
            wbf_rd_t = [0, 0]
            bank_rd = [[("v", 0)] * 3, [("v", 0)] * 3]
            for m in range(32):
                par = m % 2
                is_q = m < 16
                ngroups = [(8, 512), (520, 512)] if is_q else [(0, 512), (512, 512), (1024, 16)]
                p.wait("q", "v", stage_rd_v[par])
                cq = p.dma("q", lambda e, m=m, par=par: e.dma_start(
                    out=wstage[:, par * D:(par + 1) * D],
                    in_=qk_strips[m].rearrange("p k c -> p (k c)")))
                p.wait("v", cq)
                p.wait("v", "t", wbf_rd_t[par])
                cv_cast = p.emit("v", lambda e, par=par: _copy(e, 
                    wbf[:, par * D:(par + 1) * D], wstage[:, par * D:(par + 1) * D]))
                stage_rd_v[par] = cv_cast
                p.wait("t", "v", cv_cast)
                banks = banksets[par]
                for gi in range(len(ngroups)):
                    eng_r, cnt_r = bank_rd[par][gi]
                    p.wait("t", eng_r, cnt_r)
                last_mm = 0
                for kc in range(16):
                    for gi, (c0, wd) in enumerate(ngroups):
                        def mk(kc=kc, gi=gi, c0=c0, wd=wd, par=par, banks=banks):
                            return lambda e: e.matmul(
                                banks[gi][:, 0:wd],
                                wbf[:, par * D + kc * 128: par * D + (kc + 1) * 128],
                                lnT[:, kc * TOK + c0: kc * TOK + c0 + wd],
                                start=(kc == 0), stop=(kc == 15))
                        last_mm = p.emit("t", mk())
                wbf_rd_t[par] = last_mm
                dstT = QT if is_q else KT
                mi = m if is_q else m - 16
                for gi, (c0, wd) in enumerate(ngroups):
                    eng = "v" if (gi % 2 == 0) else "s"
                    p.wait(eng, "t", last_mm)
                    if eng == "s":
                        p.wait(eng, "g", c_zero)
                    def mkevac(gi=gi, c0=c0, wd=wd, mi=mi, dstT=dstT, banks=banks):
                        return lambda e: _copy(e, 
                            dstT[:, mi * TOKQ + c0: mi * TOKQ + c0 + wd], banks[gi][:, 0:wd])
                    c = p.emit(eng, mkevac())
                    bank_rd[par][gi] = (eng, c)
            p.barrier()
            p.realize(nc, sems)
            if KPH <= 2:
                return nc

        with (
            nc.sbuf_tensor("vstage", [128, 2 * 8 * 512], F32) as vstage,
            nc.sbuf_tensor("vwbf", [128, 2 * 16 * 512], BF16) as vwbf,
            nc.psum_tensor([128, 512], F32) as vp0,
            nc.psum_tensor([128, 512], F32) as vp1,
        ):
            vbanks = [vp0, vp1]
            stage_rd_v = [0, 0]
            vwbf_rd_t = [0, 0]
            bank_rd = [("v", 0), ("v", 0)]
            NGV = 16 * 512
            NHV = 8 * 512
            for ng in range(4):
                par = ng % 2
                cast_last = 0
                p.wait("v", "t", vwbf_rd_t[par])
                for hh in range(2):
                    slot = (ng * 2 + hh) % 2
                    p.wait("q", "v", stage_rd_v[slot])
                    cq = p.dma("q", lambda e, ng=ng, slot=slot, hh=hh: e.dma_start(
                        out=vstage[:, slot * NHV:(slot + 1) * NHV],
                        in_=v_w.rearrange("(k pp) o -> pp k o", pp=128)[
                            :, hh * 8:(hh + 1) * 8, ng * 512:(ng + 1) * 512]))
                    p.wait("v", cq)
                    def mkc(par=par, hh=hh, slot=slot):
                        return lambda e: _copy(e,
                            vwbf[:, par * NGV + hh * NHV: par * NGV + (hh + 1) * NHV],
                            vstage[:, slot * NHV:(slot + 1) * NHV])
                    cast_last = p.emit("v", mkc())
                    stage_rd_v[slot] = cast_last
                p.wait("t", "v", cast_last)
                last_mm = 0
                for vt in range(9):
                    bpar = vt % 2
                    eng_r, cnt_r = bank_rd[bpar]
                    p.wait("t", eng_r, cnt_r)
                    for kc in range(16):
                        def mk(vt=vt, kc=kc, par=par, bpar=bpar):
                            return lambda e: e.matmul(
                                vbanks[bpar][:],
                                lnT[:, kc * TOK + 128 * vt: kc * TOK + 128 * vt + 128],
                                vwbf[:, par * NGV + kc * 512: par * NGV + (kc + 1) * 512],
                                start=(kc == 0), stop=(kc == 15))
                        last_mm = p.emit("t", mk())
                    eng = "v" if (vt % 2 == 0) else "s"
                    p.wait(eng, "t", last_mm)
                    if eng == "s":
                        p.wait(eng, "g", c_zero)
                    def mkevac(vt=vt, ng=ng, bpar=bpar):
                        return lambda e: _copy(e, 
                            VTOK[:, vt * D + ng * 512: vt * D + (ng + 1) * 512],
                            vbanks[bpar][:])
                    c = p.emit(eng, mkevac())
                    bank_rd[bpar] = (eng, c)
                vwbf_rd_t[par] = last_mm
            p.barrier()
            p.realize(nc, sems)
            if KPH <= 3:
                return nc

        with (
            nc.sbuf_tensor("maskm", [128, SW], BF16) as maskm,
            nc.sbuf_tensor("maskb", [128, SW], BF16) as maskb,
            nc.sbuf_tensor("tstrip", [128, 2 * SW], F32) as tstrip,
            nc.sbuf_tensor("pstrip", [128, 2 * SW], BF16) as pstrip,
            nc.sbuf_tensor("ptbuf", [128, 4 * 1024], BF16) as ptbuf,
            nc.psum_tensor([128, 512], F32) as sp0,
            nc.psum_tensor([128, 512], F32) as sp1,
            nc.psum_tensor([128, 512], F32) as sp2,
            nc.psum_tensor([128, 512], F32) as sp3,
            nc.psum_tensor([128, 1024], BF16) as tpc0,
            nc.psum_tensor([128, 1024], BF16) as tpc1,
            nc.psum_tensor([128, 512], F32) as oc0,
            nc.psum_tensor([128, 512], F32) as oc1,
        ):
            if CPART >= 1:
                cm1 = p.dma("g", lambda e: e.dma_start(out=maskm[:], in_=masks[0]))
                cm2 = p.dma("g", lambda e: e.dma_start(out=maskb[:], in_=masks[1]))

            sbanks = [sp0, sp1, sp2, sp3]
            tbanks = [tpc0, tpc1]
            obanks = [oc0, oc1]
            s_rd = [("v", 0)] * 4
            t_rd = [("v", 0), ("v", 0)]
            o_rd = [("v", 0), ("v", 0)]
            strip_rd = [{"t": 0, "v": 0, "s": 0, "g": 0} for _ in range(2)]
            pt_rd_t = [0, 0, 0, 0]

            def c_front_a(ph):
                spar = ph % 2
                t_ap = tstrip[:, spar * SW:(spar + 1) * SW]
                ev_last = 0
                for h2 in range(2):
                    p.wait("t", "t", p.cnt["t"])
                    lo, hi = h2 * 64, h2 * 64 + 64
                    for b in range(NT):
                        sb = b % 4
                        eng_r, cnt_r = s_rd[sb]
                        p.wait("t", eng_r, cnt_r)
                        def mkmm(ph=ph, b=b, lo=lo, hi=hi, sb=sb):
                            return lambda e: e.matmul(
                                sbanks[sb][:, 0:KWIN],
                                QT[lo:hi, ph * TOKQ + 8 + 128 * b: ph * TOKQ + 8 + 128 * b + 128],
                                KT[lo:hi, ph * TOKQ + 128 * b: ph * TOKQ + 128 * b + KWIN],
                                start=True, stop=True)
                        mm_last = p.emit("t", mkmm())
                        if CPART < 1:
                            continue
                        p.wait("v", "t", mm_last)
                        p.wait("v", cm1)
                        p.wait("v", cm2)
                        p.wait("v", "g", strip_rd[spar]["g"])
                        p.wait("v", "s", strip_rd[spar]["s"])
                        def mkem(b=b, h2=h2, sb=sb, t_ap=t_ap):
                            off = (b * 2 + h2) * KWIN
                            return lambda e: e.tensor_mul(
                                t_ap[:, off:off + KWIN],
                                sbanks[sb][:, 0:KWIN],
                                maskm[:, off:off + KWIN])
                        ev_last = p.emit("v", mkem())
                        s_rd[sb] = ("v", ev_last)
                        if h2 == 1:
                            p.wait("g", "v", ev_last)
                            def mkbias(b=b, t_ap=t_ap):
                                off = b * 2 * KWIN
                                return lambda e: e.tensor_add(
                                    t_ap[:, off:off + 2 * KWIN],
                                    t_ap[:, off:off + 2 * KWIN],
                                    maskb[:, off:off + 2 * KWIN])
                            cg_bias = p.emit("g", mkbias())
                if CPART < 2:
                    return None
                strip_rd[spar]["g"] = cg_bias
                return cg_bias

            def c_front_b(ph, cg_bias):
                spar = ph % 2
                t_ap = tstrip[:, spar * SW:(spar + 1) * SW]
                p_ap = pstrip[:, spar * SW:(spar + 1) * SW]
                negmx = stats[:, 16 + 16 * spar: 32 + 16 * spar]
                rccol = stats[:, 48 + 16 * spar: 64 + 16 * spar]
                if cg_bias is None:
                    return None
                p.wait("v", "g", cg_bias)
                cv_mx = p.emit("v", lambda e, t_ap=t_ap, negmx=negmx: e.reduce_max(
                    negmx, t_ap.rearrange("p (s c) -> p s c", c=KWIN),
                    axis=AX.X, negate=True))
                p.wait("g", "v", cv_mx)
                cg_sub = p.emit("g", lambda e, t_ap=t_ap, negmx=negmx: e.tensor_add(
                    t_ap.rearrange("p (s c) -> p s c", c=KWIN),
                    t_ap.rearrange("p (s c) -> p s c", c=KWIN),
                    negmx[:, :, None].broadcast_to([128, 16, KWIN])))
                strip_rd[spar]["g"] = max(strip_rd[spar]["g"], cg_sub)
                p.wait("s", "g", cg_sub)
                cs_exp = p.emit("s", lambda e, t_ap=t_ap: e.activation(t_ap, t_ap, AF.Exp))
                strip_rd[spar]["s"] = cs_exp
                p.wait("v", "s", cs_exp)
                cv_sum = p.emit("v", lambda e, t_ap=t_ap, negmx=negmx: e.reduce_sum(
                    negmx, t_ap.rearrange("p (s c) -> p s c", c=KWIN), axis=AX.X))
                p.wait("v", "v", cv_sum)
                cv_rc = p.emit("v", lambda e, negmx=negmx, rccol=rccol: e.reciprocal(rccol, negmx))
                p.wait("g", "v", cv_rc)
                p.wait("g", "t", strip_rd[spar]["t"])
                cg_norm = p.emit("g", lambda e, t_ap=t_ap, p_ap=p_ap, rccol=rccol: e.tensor_mul(
                    p_ap.rearrange("p (s c) -> p s c", c=KWIN),
                    t_ap.rearrange("p (s c) -> p s c", c=KWIN),
                    rccol[:, :, None].broadcast_to([128, 16, KWIN])))
                strip_rd[spar]["g"] = max(strip_rd[spar]["g"], cg_norm)
                return cg_norm

            def c_back(ph, cv_norm):
                spar = ph % 2
                p_ap = pstrip[:, spar * SW:(spar + 1) * SW]
                if CPART < 3 or cv_norm is None:
                    return
                units = [(h2, b) for h2 in range(2) for b in range(NT)]
                c_ptes = [0, 0, 0, 0]
                p.wait("t", "g", cv_norm)
                p.wait("t", "t", p.cnt["t"])
                for g4 in range(4):
                    tb = g4 % 2
                    eng_r, cnt_r = t_rd[tb]
                    p.wait("t", eng_r, cnt_r)
                    tr_last = 0
                    for ui in range(4):
                        h2, b = units[g4 * 4 + ui]
                        off = (b * 2 + h2) * KWIN
                        def mktr1(off=off, ui=ui, tb=tb, p_ap=p_ap):
                            return lambda e: e.transpose(
                                tbanks[tb][:, ui * 128:(ui + 1) * 128],
                                p_ap[:, off:off + 128], ident[:])
                        p.emit("t", mktr1())
                        def mktr2(off=off, ui=ui, tb=tb, p_ap=p_ap):
                            return lambda e: e.transpose(
                                tbanks[tb][0:16, 512 + ui * 128: 512 + (ui + 1) * 128],
                                p_ap[:, off + 128:off + KWIN], ident[:])
                        tr_last = p.emit("t", mktr2())
                    strip_rd[spar]["t"] = tr_last
                    p.wait("s", "t", tr_last)
                    p.wait("s", "t", pt_rd_t[g4])
                    def mkpte1(tb=tb, g4=g4):
                        return lambda e: _copy(e,
                            ptbuf[:, g4 * 1024: g4 * 1024 + 512],
                            tbanks[tb][:, 0:512])
                    p.emit("s", mkpte1())
                    def mkpte2(tb=tb, g4=g4):
                        return lambda e: _copy(e,
                            ptbuf[0:16, g4 * 1024 + 512: g4 * 1024 + 1024],
                            tbanks[tb][0:16, 512:1024])
                    c_ptes[g4] = p.emit("s", mkpte2())
                    t_rd[tb] = ("s", c_ptes[g4])
                if CPART < 4:
                    return
                for gi, g4 in enumerate((0, 1, 2, 3)):
                    p.wait("t", "s", c_ptes[g4])
                    if gi in (0, 2):
                        p.wait("t", "t", p.cnt["t"])
                    om_last = 0
                    for ui in range(4):
                        h2, b = units[g4 * 4 + ui]
                        head = 2 * ph + h2
                        ob = b // 4
                        eng_r2, cnt_r2 = o_rd[ob]
                        p.wait("t", eng_r2, cnt_r2)
                        def mko1(ui=ui, h2=h2, b=b, head=head, ob=ob, g4=g4):
                            return lambda e: e.matmul(
                                obanks[ob][h2 * 64:(h2 + 1) * 64, (b % 4) * 128:(b % 4) * 128 + 128],
                                VTOK[:, b * D + head * HD: b * D + head * HD + HD],
                                ptbuf[:, g4 * 1024 + ui * 128: g4 * 1024 + (ui + 1) * 128],
                                start=True, stop=False)
                        p.emit("t", mko1())
                        def mko2(ui=ui, h2=h2, b=b, head=head, ob=ob, g4=g4):
                            return lambda e: e.matmul(
                                obanks[ob][h2 * 64:(h2 + 1) * 64, (b % 4) * 128:(b % 4) * 128 + 128],
                                VTOK[0:16, (b + 1) * D + head * HD: (b + 1) * D + head * HD + HD],
                                ptbuf[0:16, g4 * 1024 + 512 + ui * 128: g4 * 1024 + 512 + (ui + 1) * 128],
                                start=False, stop=True)
                        om_last = p.emit("t", mko2())
                    pt_rd_t[g4] = om_last
                    if CPART >= 5 and g4 >= 2:
                        ob = g4 - 2
                        eng = "s"
                        p.wait(eng, "t", om_last)
                        def mkoe(ob=ob, ph=ph):
                            return lambda e: _copy(e, 
                                lnT[:, ph * TOK + 8 + ob * 512: ph * TOK + 8 + ob * 512 + 512],
                                obanks[ob][:])
                        c_oe = p.emit(eng, mkoe())
                        o_rd[ob] = (eng, c_oe)

            cev = {}
            cnorm = {}
            for k in range(18):
                if k >= 2:
                    c_back(k - 2, cnorm[k - 2])
                if 1 <= k <= 16:
                    cnorm[k - 1] = c_front_b(k - 1, cev[k - 1])
                if k < 16:
                    cev[k] = c_front_a(k)
            p.barrier()
            p.realize(nc, sems)
            if KPH <= 4:
                return nc

        qkv_stack.close()

        with (
            nc.sbuf_tensor("x1", [128, NT * D], F32) as x1,
            nc.sbuf_tensor("projb", [128, D], F32) as projb,
            nc.sbuf_tensor("pwstage", [128, 2 * 8 * 512], F32) as pwstage,
            nc.sbuf_tensor("pwbf", [128, 2 * 16 * 512], BF16) as pwbf,
            nc.sbuf_tensor("xre", [128, 2 * D], F32) as xre,
            nc.sbuf_tensor("sqdump2", [128, 2 * D], F32) as sqdump2,
            nc.psum_tensor([128, 512], F32) as dp0,
            nc.psum_tensor([128, 512], F32) as dp1,
        ):
            cb = p.dma("g", lambda e: e.dma_start(out=projb[:], in_=bias_bc[0]))
            projb_q = cb
            dbanks = [dp0, dp1]
            NGP = 16 * 512
            NHS = 8 * 512
            stage_rd_v = [0, 0]
            pwbf_rd_t = [0, 0]
            bank_rd = [("v", 0), ("v", 0)]
            for ng in range(4):
                par = ng % 2
                cast_last = 0
                p.wait("v", "t", pwbf_rd_t[par])
                for hh in range(2):
                    slot = (ng * 2 + hh) % 2
                    p.wait("q", "v", stage_rd_v[slot])
                    cq = p.dma("q", lambda e, ng=ng, slot=slot, hh=hh: e.dma_start(
                        out=pwstage[:, slot * NHS:(slot + 1) * NHS],
                        in_=proj_wt.rearrange("(k pp) o -> pp k o", pp=128)[
                            :, hh * 8:(hh + 1) * 8, ng * 512:(ng + 1) * 512]))
                    p.wait("v", cq)
                    def mkc(par=par, hh=hh, slot=slot):
                        return lambda e: _copy(e, 
                            pwbf[:, par * NGP + hh * NHS: par * NGP + (hh + 1) * NHS],
                            pwstage[:, slot * NHS:(slot + 1) * NHS])
                    cast_last = p.emit("v", mkc())
                    stage_rd_v[slot] = cast_last
                p.wait("t", "v", cast_last)
                mm_last = 0
                for t in range(NT):
                    bpar = t % 2
                    eng_r, cnt_r = bank_rd[bpar]
                    p.wait("t", eng_r, cnt_r)
                    for kc in range(16):
                        def mk(t=t, kc=kc, par=par, bpar=bpar):
                            return lambda e: e.matmul(
                                dbanks[bpar][:],
                                lnT[:, kc * TOK + 8 + 128 * t: kc * TOK + 8 + 128 * t + 128],
                                pwbf[:, par * NGP + kc * 512: par * NGP + (kc + 1) * 512],
                                start=(kc == 0), stop=(kc == 15))
                        mm_last = p.emit("t", mk())
                    p.wait("v", "t", mm_last)
                    p.wait("v", projb_q)
                    def mkev(t=t, ng=ng, bpar=bpar):
                        return lambda e: e.scalar_tensor_tensor(
                            x1[:, t * D + ng * 512: t * D + (ng + 1) * 512],
                            dbanks[bpar][:], 1.0,
                            projb[:, ng * 512:(ng + 1) * 512],
                            ALU.mult, ALU.add,
                            accum_out=accD[:, t * 4 + ng: t * 4 + ng + 1])
                    c = p.emit("v", mkev())
                    bank_rd[bpar] = ("v", c)
                pwbf_rd_t[par] = mm_last
            proj_done_v = p.cnt["v"]

            xre_rd = [0, 0]
            for tpair in range(0, NT, 2):
                ts = [tpair, tpair + 1]
                xas, xres, cqs = [], [], []
                for t in ts:
                    par = t % 2
                    p.wait("q", "v", xre_rd[par])
                    cq = p.dma("q", lambda e, t=t, par=par: e.dma_start(
                        out=xre[:, par * D:(par + 1) * D],
                        in_=x_hbm[HALO + 128 * t: HALO + 128 * t + 128, :]))
                    xas.append(x1[:, t * D:(t + 1) * D])
                    xres.append(xre[:, par * D:(par + 1) * D])
                    cqs.append(cq)
                p.wait("v", "v", proj_done_v)
                p.wait("s", "v", proj_done_v)
                st4 = ln_stats2([(xas[i], ts[i] % 2,
                                  sqdump2[:, (ts[i] % 2) * D:(ts[i] % 2 + 1) * D]) for i in range(2)],
                                presums=[(accD[:, t * 4:(t + 1) * 4], []) for t in ts])
                for i, t in enumerate(ts):
                    par = t % 2
                    m_c, r_c, cs_r = st4[i]
                    p.wait("v", "s", cs_r)
                    c_c1 = 6 + par * 8
                    cv_c1 = p.emit("v", lambda e, m_c=m_c, r_c=r_c, c_c1=c_c1: e.tensor_mul(
                        st(c_c1), st(m_c), st(r_c)))
                    p.wait("v", cqs[i])
                    p.wait("v", "v", cv_c1)
                    def mke1(xa=xas[i], xr=xres[i], r_c=r_c):
                        return lambda e: e.scalar_tensor_tensor(
                            xa, xa, st(r_c), xr, ALU.mult, ALU.add)
                    cv_e1 = p.emit("v", mke1())
                    p.wait("v", "v", cv_e1)
                    def mke2(xa=xas[i], c_c1=c_c1):
                        return lambda e: e.tensor_scalar(xa, xa, st(c_c1), None, ALU.subtract)
                    cv_e2 = p.emit("v", mke2())
                    xre_rd[par] = cv_e2
                    p.wait("g", "v", cv_e2)
                    p.dma("g", lambda e, t=t, xa=xas[i]: e.dma_start(
                        out=y_hbm[128 * t:128 * t + 128, :], in_=xa))
            p.barrier()
            p.realize(nc, sems)
            if KPH <= 5:
                return nc

        big.close()

        eph = ExitStack()
        ybuf = eph.enter_context(nc.sbuf_tensor("ybuf", [128, 2 * D], F32))
        scr_e = eph.enter_context(nc.sbuf_tensor("scr_e", [128, 2 * D], F32))
        lnbf2 = eph.enter_context(nc.sbuf_tensor("lnbf2", [128, 2 * D], BF16))
        xnT = eph.enter_context(nc.sbuf_tensor("xnT", [128, 16 * 1024], BF16))
        ffnstage = eph.enter_context(nc.sbuf_tensor("ffnstage", [128, NT * D], BF16))
        fwstage = eph.enter_context(nc.sbuf_tensor("fwstage", [128, 2 * 8 * 512], F32))
        fwbf = eph.enter_context(nc.sbuf_tensor("fwbf", [128, 2 * 16 * 512], BF16))
        gam = eph.enter_context(nc.sbuf_tensor("gam", [128, D], F32))
        bet = eph.enter_context(nc.sbuf_tensor("bet", [128, D], F32))
        lnw = eph.enter_context(nc.sbuf_tensor("lnw", [128, D], BF16))
        lnb = eph.enter_context(nc.sbuf_tensor("lnb", [128, D], BF16))
        amaxc = eph.enter_context(nc.sbuf_tensor("amaxc", [128, 16], F32))
        srow = eph.enter_context(nc.sbuf_tensor("srow", [1, 1], F32))
        tpe0 = eph.enter_context(nc.psum_tensor([128, 1024], BF16))
        tpe1 = eph.enter_context(nc.psum_tensor([128, 1024], BF16))
        ep0 = eph.enter_context(nc.psum_tensor([128, 512], F32))
        ep1 = eph.enter_context(nc.psum_tensor([128, 512], F32))
        if True:
            cb_gam = p.dma("g", lambda e: e.dma_start(out=gam[:], in_=bias_bc[1]))
            cb_bet = p.dma("g", lambda e: e.dma_start(out=bet[:], in_=bias_bc[2]))
            cb_lnw = p.dma("g", lambda e: e.dma_start(out=lnw[:], in_=lnwb_bc[0]))
            cb_lnb = p.dma("g", lambda e: e.dma_start(out=lnb[:], in_=lnwb_bc[1]))
            vec_toks = [cb_gam, cb_bet, cb_lnw, cb_lnb]

            xnT_rdy = {}
            NGF = 16 * 512
            NHF = 8 * 512
            stage_rd_s = [0, 0]
            stage_rd_v = [0, 0]
            fwbf_rd_t = [0, 0]
            ebanks = [ep0, ep1]
            ebank_rd = [("v", 0), ("v", 0)]
            estate = {"amax_idx": 0, "slot": 0, "sign_last": {}}
            ffn_rdy = {}

            def emit_ffn_stage(ng):
                par = ng % 2
                sgn_last = 0
                p.wait("s", "t", fwbf_rd_t[par])
                for hh in range(2):
                    slot = estate["slot"] % 2
                    estate["slot"] += 1
                    p.wait("q", "s", stage_rd_s[slot])
                    p.wait("q", "v", stage_rd_v[slot])
                    cq = p.dma("q", lambda e, ng=ng, slot=slot, hh=hh: e.dma_start(
                        out=fwstage[:, slot * NHF:(slot + 1) * NHF],
                        in_=ffn_wt.rearrange("(k pp) o -> pp k o", pp=128)[
                            :, hh * 8:(hh + 1) * 8, ng * 512:(ng + 1) * 512]))
                    p.wait("s", cq)
                    def mks(par=par, hh=hh, slot=slot):
                        return lambda e: e.activation(
                            fwbf[:, par * NGF + hh * NHF: par * NGF + (hh + 1) * NHF],
                            fwstage[:, slot * NHF:(slot + 1) * NHF], AF.Sign)
                    sgn_last = p.emit("s", mks())
                    stage_rd_s[slot] = sgn_last
                    p.wait("v", cq)
                    idx = estate["amax_idx"]
                    estate["amax_idx"] += 1
                    cvm = p.emit("v", lambda e, slot=slot, idx=idx: e.reduce_max(
                        amaxc[:, idx:idx + 1],
                        fwstage[:, slot * NHF:(slot + 1) * NHF],
                        axis=AX.X, apply_absolute_value=True))
                    stage_rd_v[slot] = cvm
                    estate["amax_v"] = cvm
                estate["sign_last"][ng] = sgn_last

            def emit_ffn_mms(ng, tlist):
                par = ng % 2
                p.wait("t", "s", estate["sign_last"][ng])
                p.wait("t", "t", p.cnt["t"])
                mm_last = 0
                for t in tlist:
                    bpar = t % 2
                    eng_r, cnt_r = ebank_rd[bpar]
                    p.wait("t", eng_r, cnt_r)
                    for (eng_x, c_x) in xnT_rdy[t]:
                        p.wait("t", eng_x, c_x)
                    for kc in range(16):
                        def mk(t=t, kc=kc, ng=ng, par=par, bpar=bpar):
                            return lambda e: e.matmul(
                                ebanks[bpar][:],
                                xnT[:, kc * 1024 + 128 * t: kc * 1024 + 128 * t + 128],
                                fwbf[:, par * NGF + kc * 512: par * NGF + (kc + 1) * 512],
                                start=(kc == 0), stop=(kc == 15))
                        mm_last = p.emit("t", mk())
                    eng = "v" if (t % 2 == 0) else "s"
                    p.wait(eng, "t", mm_last)
                    def mkev(t=t, ng=ng, bpar=bpar):
                        return lambda e: _copy(e,
                            ffnstage[:, t * D + ng * 512: t * D + (ng + 1) * 512],
                            ebanks[bpar][:])
                    c = p.emit(eng, mkev())
                    ebank_rd[bpar] = (eng, c)
                    ffn_rdy.setdefault(t, []).append((eng, c))
                fwbf_rd_t[par] = max(fwbf_rd_t[par], mm_last)

            emit_ffn_stage(0)

            tps = [tpe0, tpe1]
            bank_rd = [("v", 0), ("v", 0)]
            lnbf_rd_t = [0, 0]
            scr_rd_g = [0, 0]
            ybuf_rd = [0, 0]
            for tpair in range(0, NT, 2):
                ts = [tpair, tpair + 1]
                yas, scrs, lbs, cqs = [], [], [], []
                for t in ts:
                    par = t % 2
                    ya = ybuf[:, par * D:(par + 1) * D]
                    p.wait("q", "v", ybuf_rd[par])
                    cq = p.dma("q", lambda e, t=t, ya=ya: e.dma_start(
                        out=ya, in_=y_hbm[128 * t:128 * t + 128, :]))
                    yas.append(ya)
                    scrs.append(scr_e[:, par * D:(par + 1) * D])
                    lbs.append(lnbf2[:, par * D:(par + 1) * D])
                    cqs.append(cq)
                for i, t in enumerate(ts):
                    p.wait("v", cqs[i])
                    p.wait("s", cqs[i])
                    p.wait("s", "t", lnbf_rd_t[t % 2])
                st1 = ln_stats2([(yas[i], ts[i] % 2, lbs[i]) for i in range(2)])
                cv_ln2 = []
                for i, t in enumerate(ts):
                    m_c, r_c, cs_r = st1[i]
                    p.wait("v", "s", cs_r)
                    p.wait("v", "g", scr_rd_g[t % 2])
                    cv_ln2.append(p.emit("v", lambda e, scr=scrs[i], ya=yas[i], m_c=m_c, r_c=r_c:
                                         e.tensor_scalar(scr, ya, st(m_c), st(r_c),
                                                         ALU.subtract, ALU.mult)))
                    ybuf_rd[t % 2] = cv_ln2[i]
                for i in range(2):
                    p.wait("s", "v", cv_ln2[i])
                    p.wait("v", "v", cv_ln2[i])
                st2 = ln_stats2([(scrs[i], ts[i] % 2, lbs[i]) for i in range(2)])
                cv_ln3 = []
                for i, t in enumerate(ts):
                    m_c2, r_c2, cs_r2 = st2[i]
                    p.wait("v", "s", cs_r2)
                    cv_ln3.append(p.emit("v", lambda e, scr=scrs[i], m_c2=m_c2, r_c2=r_c2:
                                         e.tensor_scalar(scr, scr, st(m_c2), st(r_c2),
                                                         ALU.subtract, ALU.mult)))
                cv_w = []
                for i, t in enumerate(ts):
                    p.wait("v", "v", cv_ln3[i])
                    p.wait("v", cb_lnw)
                    cv_w.append(p.emit("v", lambda e, scr=scrs[i]: e.tensor_mul(scr, scr, lnw[:])))
                    scr_rd_g[t % 2] = 0
                cv_aff = []
                for i, t in enumerate(ts):
                    p.wait("v", "v", cv_w[i])
                    p.wait("v", cb_lnb)
                    p.wait("v", "t", lnbf_rd_t[t % 2])
                    cv_aff.append(p.emit("v", lambda e, lb=lbs[i], scr=scrs[i]:
                                         e.tensor_add(lb, scr, lnb[:])))
                for i, t in enumerate(ts):
                    last_t = 0
                    rdy = []
                    p.wait("t", "t", p.cnt["t"])
                    for half in range(2):
                        bank = tps[half]
                        eng_r, cnt_r = bank_rd[half]
                        p.wait("t", "v", cv_aff[i])
                        p.wait("t", eng_r, cnt_r)
                        p.wait("t", c_const)
                        for k8 in range(8):
                            kc = half * 8 + k8
                            def mktr(kc=kc, k8=k8, bank=bank, lb=lbs[i]):
                                return lambda e: e.transpose(
                                    bank[:, k8 * 128:(k8 + 1) * 128],
                                    lb[:, kc * 128:(kc + 1) * 128], ident[:])
                            last_t = p.emit("t", mktr())
                        eng = "v" if half == 0 else "s"
                        p.wait(eng, "t", last_t)
                        def mkev(half=half, t=t, bank=bank):
                            srcv = bank[:].rearrange("p (k c) -> p k c", c=128)
                            dst = xnT[:].rearrange("p (k c) -> p k c", c=1024)[
                                :, half * 8:(half + 1) * 8, 128 * t: 128 * t + 128]
                            return lambda e: _copy(e, dst, srcv)
                        c = p.emit(eng, mkev())
                        bank_rd[half] = (eng, c)
                        rdy.append((eng, c))
                    lnbf_rd_t[t % 2] = last_t
                    xnT_rdy[t] = rdy
                emit_ffn_mms(0, ts)
            e1_t_done = p.cnt["t"]
            e1_v_done = p.cnt["v"]
            e1_s_done = p.cnt["s"]

            for ng in range(1, 3):
                emit_ffn_stage(ng)
                emit_ffn_mms(ng, list(range(NT)))
            emit_ffn_stage(3)

            p.wait("v", "v", estate["amax_v"])
            cv_col = p.emit("v", lambda e: e.reduce_max(st(7), amaxc[:, 0:8], axis=AX.X))
            p.wait("g", "v", cv_col)
            cg_red = p.emit("g", lambda e: e.tensor_reduce(
                srow[0:1, :], st(7), axis=AX.C, op=ALU.max))
            p.wait("q", "g", cg_red)
            cq_s1 = p.dma("q", lambda e: e.dma_start(out=sc_scr[None, :], in_=srow[:]))
            p.wait("q", cq_s1)
            cq_s2 = p.dma("q", lambda e: e.dma_start(
                out=scalev[:], in_=sc_scr[None, :].to_broadcast((128, 1))))
            p.wait("v", cq_s2)
            p.wait("v", cb_gam)
            cv_gs = p.emit("v", lambda e: e.tensor_scalar(
                gam[:], gam[:], scalev[:], None, ALU.mult))

            ybuf_rd2 = [0, 0]
            scr_st_gd = [0, 0]
            ffn_done_pair = {}
            for tpair in range(0, NT, 2):
                emit_ffn_mms(3, [tpair, tpair + 1])
                ffn_done_pair[tpair] = (p.cnt["v"], p.cnt["s"])
            for tpair in range(0, NT, 2):
                ts = [tpair, tpair + 1]
                ffn_v_done, ffn_s_done = ffn_done_pair[tpair]
                scrs, sqds, yas, cqs, cvg, cgb = [], [], [], [], [], []
                for t in ts:
                    par = t % 2
                    ya = ybuf[:, par * D:(par + 1) * D]
                    p.wait("q", "v", max(ybuf_rd2[par], e1_v_done))
                    cq = p.dma("q", lambda e, t=t, ya=ya: e.dma_start(
                        out=ya, in_=y_hbm[128 * t:128 * t + 128, :]))
                    scrs.append(scr_e[:, par * D:(par + 1) * D])
                    sqds.append(lnbf2[:, par * D:(par + 1) * D])
                    yas.append(ya)
                    cqs.append(cq)
                for i, t in enumerate(ts):
                    par = t % 2
                    p.wait("v", "v", max(ffn_v_done, cv_gs))
                    p.wait("v", "s", ffn_s_done)
                    p.wait("v", "g", scr_rd_g[par])
                    p.wait("v", scr_st_gd[par])
                    cvg.append(p.emit("v", lambda e, t=t, scr=scrs[i]: e.tensor_mul(
                        scr, ffnstage[:, t * D:(t + 1) * D], gam[:])))
                for i, t in enumerate(ts):
                    p.wait("v", "v", cvg[i])
                    p.wait("v", cb_bet)
                    cgb.append(p.emit("v", lambda e, scr=scrs[i]: e.tensor_add(scr, scr, bet[:])))
                for i, t in enumerate(ts):
                    p.wait("s", "v", cgb[i])
                    p.wait("s", "t", e1_t_done)
                st3 = ln_stats2([(scrs[i], ts[i] % 2, sqds[i]) for i in range(2)])
                for i, t in enumerate(ts):
                    par = t % 2
                    m_c, r_c, cs_r = st3[i]
                    p.wait("v", "s", cs_r)
                    c_c1 = 6 + par * 8
                    cv_c1 = p.emit("v", lambda e, m_c=m_c, r_c=r_c, c_c1=c_c1: e.tensor_mul(
                        st(c_c1), st(m_c), st(r_c)))
                    p.wait("v", cqs[i])
                    p.wait("v", "v", cv_c1)
                    def mkf1(scr=scrs[i], ya=yas[i], r_c=r_c):
                        return lambda e: e.scalar_tensor_tensor(
                            scr, scr, st(r_c), ya, ALU.mult, ALU.add)
                    cv_f1 = p.emit("v", mkf1())
                    ybuf_rd2[par] = cv_f1
                    p.wait("v", "v", cv_f1)
                    def mkf2(scr=scrs[i], c_c1=c_c1):
                        return lambda e: e.tensor_scalar(scr, scr, st(c_c1), None, ALU.subtract)
                    cv_f2 = p.emit("v", mkf2())
                    p.wait("g", "v", cv_f2)
                    out_store = p.dma("g", lambda e, t=t, scr=scrs[i]: e.dma_start(
                        out=out_hbm[128 * t:128 * t + 128, :], in_=scr))
                    scr_st_gd[par] = out_store

            n = p.dma_idx["gd"]
            for cls in range(GDPOOL):
                k = (n - cls + GDPOOL - 1) // GDPOOL
                if k:
                    p.ops["g"].append(("wait", ("gd", cls), 16 * k))
            p.realize(nc, sems)
        eph.close()

    semstack.close()
    stack.close()
    return nc


_NC_CACHE = {}


def _host_prep(L0_attn_x, L0_attn_qkv_w, L0_attn_proj_w, L0_attn_proj_b,
               L0_ffn_weight, L0_ffn_gamma, L0_ffn_beta, L0_ffn_ln_weight,
               L0_ffn_ln_bias):
    bf = ml_dtypes.bfloat16
    x = np.ascontiguousarray(np.asarray(L0_attn_x, dtype=np.float32)[0])
    qkv_w = np.asarray(L0_attn_qkv_w, dtype=np.float32)
    W2 = np.ascontiguousarray(qkv_w[:4096].T)
    qk_strips = np.ascontiguousarray(
        W2.reshape(16, 128, 32, 128).transpose(2, 1, 0, 3))
    v_w = np.ascontiguousarray(qkv_w[4096:].T)
    proj_wt = np.ascontiguousarray(np.asarray(L0_attn_proj_w, np.float32).T)
    ffn_wt = np.ascontiguousarray(np.asarray(L0_ffn_weight, np.float32).T)
    bias_bc = np.ascontiguousarray(np.stack([
        np.broadcast_to(np.asarray(L0_attn_proj_b, np.float32), (128, D)),
        np.broadcast_to(np.asarray(L0_ffn_gamma, np.float32), (128, D)),
        np.broadcast_to(np.asarray(L0_ffn_beta, np.float32), (128, D)),
    ]))
    lnwb_bc = np.ascontiguousarray(np.stack([
        np.broadcast_to(np.asarray(L0_ffn_ln_weight, np.float32), (128, D)),
        np.broadcast_to(np.asarray(L0_ffn_ln_bias, np.float32), (128, D)),
    ]).astype(bf))
    ident = np.eye(128, dtype=bf)

    xp = np.zeros((SEQ + 2 * HALO, D), np.float32)
    xp[HALO:HALO + SEQ] = x
    x_shards = [np.ascontiguousarray(xp[c * OWN: c * OWN + OWN + 2 * HALO])
                for c in range(NCORES)]

    mask_list = []
    pcol = np.arange(128)[:, None]
    jcol = np.arange(KWIN)[None, :]
    for c in range(NCORES):
        mm = np.zeros((128, SW), np.float32)
        mb = np.full((128, SW), NEG, np.float32)
        for b in range(NT):
            q = c * OWN + b * 128 + pcol
            k = c * OWN + b * 128 - 8 + jcol
            d = np.abs(q - k).astype(np.float32)
            inw = (k >= q - 8) & (k < q + 8) & (k >= 0) & (k < SEQ)
            mval = np.where(inw, (8.0 - d / 2.0) / 8.0, 0.0) * 0.125
            bval = np.where(inw, 0.0, NEG)
            for h2 in range(2):
                mm[:, (b * 2 + h2) * KWIN:(b * 2 + h2 + 1) * KWIN] = mval
                mb[:, (b * 2 + h2) * KWIN:(b * 2 + h2 + 1) * KWIN] = bval
        mask_list.append(np.ascontiguousarray(np.stack([mm, mb]).astype(bf)))

    shared = {
        "qk_strips": qk_strips, "v_w": v_w, "proj_wt": proj_wt,
        "ffn_wt": ffn_wt, "bias_bc": bias_bc, "lnwb_bc": lnwb_bc,
        "ident": ident,
    }
    in_maps = []
    for c in range(NCORES):
        m = dict(shared)
        m["x_shard"] = x_shards[c]
        m["masks"] = mask_list[c]
        in_maps.append(m)
    return in_maps


def kernel(**inputs):
    in_maps = _host_prep(**inputs)
    if "nc" not in _NC_CACHE:
        _NC_CACHE["nc"] = build_program()
    nc = _NC_CACHE["nc"]
    res = run_bass_kernel_spmd(nc, in_maps, core_ids=list(range(NCORES)))
    out = np.concatenate([np.asarray(res.results[c]["out"]) for c in range(NCORES)], axis=0)
    return out.reshape(1, SEQ, D).astype(np.float32)
